# version 08
# baseline (speedup 1.0000x reference)
import numpy as np
import concourse.bass as bass
import concourse.mybir as mybir
from concourse.bass_utils import run_bass_kernel_spmd

F32 = mybir.dt.float32
BF16 = mybir.dt.bfloat16
AF = mybir.ActivationFunctionType
ALU = mybir.AluOpType
AX = mybir.AxisListType

B, D = 4, 1024
NH, DH = 16, 64
E = 8
ROPE_BASE = 10000.0

ENGS = ("sync", "gpsimd", "tensor", "vector", "scalar")
C_EPS, C_R1024, C_NHALF, C_R8, C_SHIFT, C_CAP = 0, 1, 2, 3, 4, 5


class Prog:

    def __init__(self, nc, es):
        self.nc = nc
        self.q = {e: [] for e in ENGS}
        self.ctr = {}
        self.h = {}
        self.es = es

    def sem(self, name):
        self.h[name] = self.es.enter_context(self.nc.semaphore(name))
        self.ctr[name] = 0

    def add(self, eng, fn):
        self.q[eng].append(fn)

    def inc(self, eng, name, mk, k=1):
        self.ctr[name] += k
        h, kk = self.h[name], k
        self.add(eng, lambda e: mk(e).then_inc(h, kk))
        return self.ctr[name]

    def op(self, eng, mk):
        self.add(eng, lambda e: mk(e))

    def wait(self, eng, name, v=None):
        h = self.h[name]
        v = self.ctr[name] if v is None else v
        assert v >= 0
        self.q[eng].append(lambda e, h=h, v=v: e.wait_ge(h, v))
        return v

    def emit(self, block):
        for eng in ENGS:
            ops = self.q[eng]
            getattr(block, eng)(lambda e, ops=ops: [f(e) for f in ops])


def build_nc(T):
    TOK = T
    OWN = T // 2
    NT = TOK // 128
    NOT = OWN // 128
    GT = (4 * T) // 128
    QB = 512
    NQB = TOK // QB
    NF = TOK // 128
    assert GT <= 128

    nc = bass.Bass("TRN2", target_bir_lowering=False)
    dt = lambda n, s, d=F32, **kw: nc.dram_tensor(n, s, d, **kw).ap()

    x_in = dt("x_in", [TOK, D], kind="ExternalInput")
    x_own = dt("x_own", [OWN, D], kind="ExternalInput")
    wq = dt("wq", [D, 512], kind="ExternalInput")
    wk = dt("wk", [D, 512], kind="ExternalInput")
    wv = dt("wv", [D, 512], kind="ExternalInput")
    wo_i = dt("wo_i", [D, D], kind="ExternalInput")
    aq = dt("aq", [D, E], kind="ExternalInput")
    av = dt("av", [D, E], kind="ExternalInput")
    bq_i = dt("bq_i", [E, 512], kind="ExternalInput")
    bv_i = dt("bv_i", [E, 512], kind="ExternalInput")
    wg_i = dt("wg_i", [D, E], kind="ExternalInput")
    be_i = dt("be_i", [1, D], kind="ExternalInput")
    we_i = dt("we_i", [D, D], kind="ExternalInput")
    th_i = dt("th_i", [128, 4 * (TOK // 128) * 32], kind="ExternalInput")
    tri_i = dt("tri_i", [128, 128], kind="ExternalInput")
    trs_i = dt("trs_i", [128, 128], kind="ExternalInput")
    nid_i = dt("nid_i", [128, 128], kind="ExternalInput")
    one_i = dt("one_i", [1, 128], kind="ExternalInput")
    cst_i = dt("cst_i", [128, 16], kind="ExternalInput")
    out_o = dt("out_o", [OWN, D], kind="ExternalOutput")

    sh_o = dt("sh_o", [D, TOK], BF16, kind="Internal", addr_space="Shared")
    flag_i = dt("flag_i", [1, 2], kind="Internal")
    flag_o = dt("flag_o", [2, 2], kind="Internal")
    lg_i = dt("lg_i", [OWN, E], kind="Internal")
    lg_o = dt("lg_o", [4 * T, E], kind="Internal")
    scr = dt("scr", [GT, E], kind="Internal")
    scr2 = dt("scr2", [GT, E], kind="Internal")
    gate_d = dt("gate_d", [128, GT], kind="Internal")

    from contextlib import ExitStack
    with ExitStack() as es0:
        sb0 = lambda n, s, d=F32, **kw: es0.enter_context(nc.sbuf_tensor(n, s, d, **kw))

        s_cst = sb0("s_cst", [128, 16])
        s_one = sb0("s_one", [1, 128])
        s_tri = sb0("s_tri", [128, 2, 128])
        s_trs = sb0("s_trs", [128, 128], BF16)
        s_nid = sb0("s_nid", [128, 128], BF16)
        s_x = sb0("s_x", [128, 2, D])
        s_st = sb0("s_st", [128, 2, 8])
        s_wo = sb0("s_wo", [128, 8, D], BF16)
        s_wg = sb0("s_wg", [128, 8, E], BF16)
        s_be = sb0("s_be", [1, D])
        s_sc = sb0("s_sc", [128, 2, NF, 32])

        p_all = es0.enter_context(nc.psum_tensor("p_all", [128, 8, 512], F32))
        bank = lambda i, lo=0, hi=512: p_all[:, i, lo:hi]

        pr = Prog(nc, es0)
        for s in ("SWA", "SWB", "SWC", "SWF", "SXX0", "SXX1", "STRL0", "STRL1",
                  "SXQ0", "SXQ1", "SXO", "SXA", "SXL", "SXG", "SXR", "SWD",
                  "ST0", "ST1", "MM", "VE", "SC", "GP"):
            pr.sem(s)
        cc = lambda i: s_cst[:, i:i + 1]

        def g_dma(dst, src, sem="SWA"):
            return pr.inc("gpsimd", sem,
                          lambda e, d=dst, s=src: e.dma_start(out=d, in_=s), 16)

        g_dma(s_cst[:], cst_i)
        g_dma(s_tri[:, 0], tri_i)
        g_dma(s_tri[:, 1], trs_i)
        g_dma(s_trs[:], trs_i)
        g_dma(s_nid[:], nid_i)
        g_dma(s_one[:], one_i)
        g_dma(s_wg[:], wg_i.rearrange("(c p) j -> p c j", p=128))
        g_dma(s_be[:], be_i)

        with ExitStack() as es1:
            sb1 = lambda n, s, d=F32, **kw: es1.enter_context(nc.sbuf_tensor(n, s, d, **kw))
            s_th = sb1("s_th", [128, 4, NF, 32])
            s_tm = sb1("s_tm", [128, 2, NF, 32])
            v_th = g_dma(s_th[:].rearrange("p k n f -> p (k n f)"), th_i)

            pr.wait("scalar", "SWA", v_th)
            pr.inc("scalar", "SC", lambda e: e.activation(
                out=s_sc[:, 0], in_=s_th[:, 0], func=AF.Sin))
            v_raw = pr.inc("scalar", "SC", lambda e: e.activation(
                out=s_sc[:, 1], in_=s_th[:, 2], func=AF.Sin))
            pr.wait("vector", "SC", v_raw)
            pr.op("vector", lambda e: e.tensor_tensor(
                out=s_tm[:, 0], in0=s_th[:, 1], in1=s_sc[:, 1], op=ALU.mult))
            v_m = pr.inc("vector", "VE", lambda e: e.tensor_tensor(
                out=s_tm[:, 1], in0=s_th[:, 3], in1=s_sc[:, 0], op=ALU.mult))
            pr.wait("vector", "VE", v_m)
            pr.op("vector", lambda e: e.tensor_tensor(
                out=s_sc[:, 0], in0=s_sc[:, 0], in1=s_tm[:, 0], op=ALU.add))
            v_tab = pr.inc("vector", "VE", lambda e: e.tensor_tensor(
                out=s_sc[:, 1], in0=s_sc[:, 1], in1=s_tm[:, 1], op=ALU.subtract))

        with ExitStack() as es2:
            sb2 = lambda n, s, d=F32, **kw: es2.enter_context(nc.sbuf_tensor(n, s, d, **kw))
            s_sq = sb2("s_sq", [128, D])
            s_ln = sb2("s_ln", [128, 2, D], BF16)
            s_lnf = sb2("s_lnf", [128, 2, 8, 128], BF16, is_transpose_dest=True)
            s_wq = sb2("s_wq", [128, 8, 512], BF16)
            s_wk = sb2("s_wk", [128, 8, 512], BF16)
            s_wv = sb2("s_wv", [128, 8, 512], BF16)
            s_aqv = sb2("s_aqv", [128, 8, 40], BF16)
            s_bq = sb2("s_bq", [E, 512])
            s_bv = sb2("s_bv", [E, 512])
            s_b4 = sb2("s_b4", [64, 512], BF16)
            s_t1 = sb2("s_t1", [64, 2, 128], BF16)
            s_rm = sb2("s_rm", [128, 4, 8, 32])
            s_qtm = sb2("s_qtm", [128, 2, 512], BF16)
            s_ktm = sb2("s_ktm", [128, 2, 512], BF16)
            s_qf = sb2("s_qf", [128, 4, TOK], BF16, is_transpose_dest=True)
            s_kf = sb2("s_kf", [128, 4, TOK], BF16, is_transpose_dest=True)
            s_v = sb2("s_v", [128, NT, 8, 65], BF16)
            s_pt = sb2("s_pt", [128, 2, 2, 512], BF16)
            s_rcp = sb2("s_rcp", [1, 512])
            s_rb = sb2("s_rb", [64, 512])

            pr.wait("gpsimd", "VE", v_tab)
            g_dma(s_wq[:], wq.rearrange("(c p) j -> p c j", p=128), "SWB")
            g_dma(s_wk[:], wk.rearrange("(c p) j -> p c j", p=128), "SWB")
            g_dma(s_wv[:], wv.rearrange("(c p) j -> p c j", p=128), "SWB")
            g_dma(s_aqv[:, :, 0:8], aq.rearrange("(c p) r -> p c r", p=128), "SWB")
            g_dma(s_aqv[:, :, 32:40], av.rearrange("(c p) r -> p c r", p=128), "SWB")
            g_dma(s_bq[:], bq_i, "SWB")
            v_w1 = g_dma(s_bv[:], bv_i, "SWB")
            v_wo = g_dma(s_wo[:], wo_i.rearrange("(c p) j -> p c j", p=128), "SWC")

            pr.wait("vector", "SWB", v_w1)
            pr.op("vector", lambda e: e.memset(s_aqv[:, :, 8:32], 0.0))
            pr.op("vector", lambda e: e.tensor_scalar_mul(
                out=s_b4[0:8, :], in0=s_bq[:], scalar1=4.0))
            pr.op("vector", lambda e: e.tensor_scalar_mul(
                out=s_b4[32:40, :], in0=s_bv[:], scalar1=4.0))
            v_vone = pr.inc("vector", "VE", lambda e: e.memset(s_v[:, :, :, 64:65], 1.0))

            xr = x_in.rearrange("(n p) d -> n p d", p=128)
            vSC_ln = [0] * NT
            vSC_vev = [0] * NT
            vVE_rope = [0] * NT
            vSX_lnf = [0] * NT
            vSX_qk = [0] * NT
            vMM_qkv = [0] * NT
            vMM_t1 = [0] * NT
            vVE_t1e = [0] * NT

            for i in range(NT):
                par = i % 2
                bq_b, bk_b, bv_b, bt_b = 4 * par + 0, 4 * par + 1, 4 * par + 2, 4 * par + 3

                if i >= 2:
                    pr.wait("sync", "SC", vSC_ln[i - 2])
                vSX_x = pr.inc("sync", "SXX%d" % par,
                               lambda e, i=i, par=par: e.dma_start(
                                   out=s_x[:, par], in_=xr[i]), 16)

                pr.wait("vector", "SXX%d" % par, vSX_x)
                v1 = pr.inc("vector", "VE", lambda e, par=par: e.reduce_sum(
                    out=s_st[:, par, 0:1], in_=s_x[:, par], axis=AX.X, negate=True))
                pr.wait("vector", "VE", v1)
                v2 = pr.inc("vector", "VE", lambda e, par=par: e.tensor_scalar_mul(
                    out=s_st[:, par, 1:2], in0=s_st[:, par, 0:1], scalar1=1.0 / D))
                pr.wait("vector", "VE", v2)
                v_xm = pr.inc("vector", "VE", lambda e, par=par: e.tensor_scalar_add(
                    out=s_x[:, par], in0=s_x[:, par], scalar1=s_st[:, par, 1:2]))
                pr.wait("scalar", "VE", v_xm)
                s1 = pr.inc("scalar", "SC", lambda e, par=par: e.activation(
                    out=s_sq[:], in_=s_x[:, par], func=AF.Square,
                    accum_out=s_st[:, par, 2:3]))
                pr.wait("scalar", "SC", s1)
                s2 = pr.inc("scalar", "SC", lambda e, par=par: e.activation(
                    out=s_st[:, par, 3:4], in_=s_st[:, par, 2:3], func=AF.Ln,
                    scale=cc(C_R1024), bias=cc(C_EPS)))
                pr.wait("scalar", "SC", s2)
                s3 = pr.inc("scalar", "SC", lambda e, par=par: e.activation(
                    out=s_st[:, par, 4:5], in_=s_st[:, par, 3:4], func=AF.Exp,
                    scale=cc(C_NHALF)))
                pr.wait("scalar", "SC", s3)
                if i >= 2:
                    pr.wait("scalar", "STRL%d" % par, vSX_lnf[i - 2])
                vSC_ln[i] = pr.inc("scalar", "SC", lambda e, par=par: e.activation(
                    out=s_ln[:, par], in_=s_x[:, par], func=AF.Copy,
                    scale=s_st[:, par, 4:5]))

                pr.wait("scalar", "SC", vSC_ln[i])
                if i >= 2:
                    pr.wait("scalar", "MM", vMM_qkv[i - 2])
                vSX_lnf[i] = pr.inc("scalar", "STRL%d" % par, lambda e, par=par: e.dma_start(
                    out=s_lnf[:, par], in_=s_ln[:, par], transpose=True), 16)

                pr.wait("tensor", "STRL%d" % par, vSX_lnf[i])
                if i == 0:
                    pr.wait("tensor", "SWB", v_w1)
                    pr.wait("tensor", "SWA", None)
                    pr.wait("tensor", "VE", v_vone)
                if i >= 2:
                    pr.wait("tensor", "VE", vVE_rope[i - 2])
                    pr.wait("tensor", "SC", vSC_vev[i - 2])
                    pr.wait("tensor", "VE", vVE_t1e[i - 2])
                for c in range(7):
                    pr.op("tensor", lambda e, c=c, par=par, bt=bt_b: e.matmul(
                        p_all[0:40, bt, 0:128], lhsT=s_aqv[:, c, :],
                        rhs=s_lnf[:, par, c, :], start=(c == 0), stop=False))
                vMM_t1[i] = pr.inc("tensor", "MM", lambda e, par=par, bt=bt_b: e.matmul(
                    p_all[0:40, bt, 0:128], lhsT=s_aqv[:, 7, :],
                    rhs=s_lnf[:, par, 7, :], start=False, stop=True))
                for c in range(8):
                    pr.op("tensor", lambda e, c=c, par=par, b=bq_b: e.matmul(
                        bank(b), lhsT=s_lnf[:, par, c, :], rhs=s_wq[:, c, :],
                        start=(c == 0), stop=False))
                    pr.op("tensor", lambda e, c=c, par=par, b=bk_b: e.matmul(
                        bank(b), lhsT=s_lnf[:, par, c, :], rhs=s_wk[:, c, :],
                        start=(c == 0), stop=(c == 7)))
                    pr.op("tensor", lambda e, c=c, par=par, b=bv_b: e.matmul(
                        bank(b), lhsT=s_lnf[:, par, c, :], rhs=s_wv[:, c, :],
                        start=(c == 0), stop=False))
                pr.wait("vector", "MM", vMM_t1[i])
                vVE_t1e[i] = pr.inc("vector", "VE", lambda e, par=par, bt=bt_b: e.tensor_copy(
                    out=s_t1[0:40, par, :], in_=p_all[0:40, bt, 0:128]))
                pr.wait("tensor", "VE", vVE_t1e[i])
                pr.op("tensor", lambda e, par=par, b=bq_b: e.matmul(
                    bank(b), lhsT=s_t1[0:8, par, :], rhs=s_b4[0:8, :],
                    start=False, stop=True))
                vMM_qkv[i] = pr.inc("tensor", "MM", lambda e, par=par, b=bv_b: e.matmul(
                    bank(b), lhsT=s_t1[32:40, par, :], rhs=s_b4[32:40, :],
                    start=False, stop=True))

                if i == 0:
                    pr.wait("vector", "VE", v_tab)
                pr.wait("vector", "MM", vMM_qkv[i])
                if i >= 2:
                    pr.wait("vector", "SXQ%d" % par, vSX_qk[i - 2])
                co = s_sc[:, 1, i:i + 1, :].to_broadcast((128, 8, 32))
                si = s_sc[:, 0, i:i + 1, :].to_broadcast((128, 8, 32))
                for (bnk, stm) in ((bq_b, s_qtm), (bk_b, s_ktm)):
                    pv = p_all[:, bnk, :].rearrange("p (h c) -> p h c", h=8)
                    qE, qO = pv[:, :, 0:32], pv[:, :, 32:64]
                    ot = stm[:, par, :].rearrange("p (h c) -> p h c", h=8)
                    pr.op("vector", lambda e, qE=qE, co=co: e.tensor_tensor(
                        out=s_rm[:, 0], in0=qE, in1=co, op=ALU.mult))
                    pr.op("vector", lambda e, qO=qO, si=si: e.tensor_tensor(
                        out=s_rm[:, 1], in0=qO, in1=si, op=ALU.mult))
                    pr.op("vector", lambda e, qE=qE, si=si: e.tensor_tensor(
                        out=s_rm[:, 2], in0=qE, in1=si, op=ALU.mult))
                    v3 = pr.inc("vector", "VE", lambda e, qO=qO, co=co: e.tensor_tensor(
                        out=s_rm[:, 3], in0=qO, in1=co, op=ALU.mult))
                    pr.wait("vector", "VE", v3)
                    pr.op("vector", lambda e, ot=ot: e.tensor_tensor(
                        out=ot[:, :, 0:32], in0=s_rm[:, 0], in1=s_rm[:, 1],
                        op=ALU.subtract))
                    v4 = pr.inc("vector", "VE", lambda e, ot=ot: e.tensor_tensor(
                        out=ot[:, :, 32:64], in0=s_rm[:, 2], in1=s_rm[:, 3],
                        op=ALU.add))
                    pr.wait("vector", "VE", v4)
                vVE_rope[i] = v4
                pr.wait("scalar", "MM", vMM_qkv[i])
                vSC_vev[i] = pr.inc("scalar", "SC", lambda e, par=par, i=i, b=bv_b: e.activation(
                    out=s_v[:, i, :, 0:64],
                    in_=bank(b).rearrange("p (h c) -> p h c", h=8), func=AF.Copy))

                pr.wait("sync", "VE", vVE_rope[i])
                pr.inc("sync", "SXQ%d" % par, lambda e, par=par, i=i: e.dma_start(
                    out=s_qf[:, :, 128 * i:128 * (i + 1)], in_=s_qtm[:, par, :],
                    transpose=True), 16)
                vSX_qk[i] = pr.inc("sync", "SXQ%d" % par, lambda e, par=par, i=i: e.dma_start(
                    out=s_kf[:, :, 128 * i:128 * (i + 1)], in_=s_ktm[:, par, :],
                    transpose=True), 16)

            vSC_p1 = pr.ctr["SC"]

            vSC_exp = {}
            vMM_att = {}
            vVE_on = {}
            pr.wait("tensor", "SXQ0", None)
            pr.wait("tensor", "SXQ1", None)
            pr.wait("tensor", "SC", vSC_p1)
            for hp in range(4):
                for qb in range(NQB):
                    q0 = qb * QB
                    nkt = q0 // 128 + 4
                    for kt in range(nkt):
                        par = kt % 2
                        koff = kt * 128
                        qlo = max(0, koff - q0)
                        if (hp, qb, kt - 2) in vSC_exp:
                            pr.wait("tensor", "SC", vSC_exp[(hp, qb, kt - 2)])
                        elif qb > 0 and kt < 2:
                            k_prev = (q0 - QB) // 128 + 4
                            if (hp, qb - 1, k_prev - 2 + kt) in vSC_exp:
                                pr.wait("tensor", "SC",
                                        vSC_exp[(hp, qb - 1, k_prev - 2 + kt)])
                        elif hp > 0 and qb == 0 and kt < 2:
                            pr.wait("tensor", "SC",
                                    vSC_exp[(hp - 1, NQB - 1, (NQB - 1) * QB // 128 + 2 + kt)])
                        diag = koff >= q0
                        vS = None
                        for h in range(2):
                            mk1 = lambda e, hp=hp, h=h, par=par, koff=koff, \
                                  q0=q0, qlo=qlo, diag=diag: e.matmul(
                                p_all[:, 2 * par + h, qlo:512],
                                lhsT=s_kf[64 * h:64 * h + 64, hp, koff:koff + 128],
                                rhs=s_qf[64 * h:64 * h + 64, hp, q0 + qlo:q0 + 512],
                                start=True, stop=not diag)
                            mk2 = lambda e, h=h, par=par, qlo=qlo: e.matmul(
                                p_all[:, 2 * par + h, qlo:qlo + 128],
                                lhsT=s_trs[:], rhs=s_nid[:],
                                start=False, stop=True)
                            if diag:
                                pr.op("tensor", mk1)
                                if h == 1:
                                    vS = pr.inc("tensor", "MM", mk2)
                                else:
                                    pr.op("tensor", mk2)
                            else:
                                if h == 1:
                                    vS = pr.inc("tensor", "MM", mk1)
                                else:
                                    pr.op("tensor", mk1)
                        pr.wait("scalar", "MM", vS)
                        if (hp, qb, kt - 2) in vMM_att:
                            pr.wait("scalar", "MM", vMM_att[(hp, qb, kt - 2)])
                        vSC_exp[(hp, qb, kt)] = pr.inc("scalar", "SC",
                            lambda e, par=par, qlo=qlo: e.activation(
                                out=s_pt[:, par, :, qlo:512],
                                in_=p_all[:, 2 * par:2 * par + 2, qlo:512],
                                func=AF.Exp, scale=cc(C_R8), bias=cc(C_SHIFT)))
                        pr.wait("tensor", "SC", vSC_exp[(hp, qb, kt)])
                        if kt == 0 and (hp, qb - 1) in vVE_on:
                            pr.wait("tensor", "VE", vVE_on[(hp, qb - 1)])
                        if kt == 0 and qb == 0 and (hp - 1, NQB - 1) in vVE_on:
                            pr.wait("tensor", "VE", vVE_on[(hp - 1, NQB - 1)])
                        for h in range(2):
                            mk = lambda e, hp=hp, h=h, par=par, kt=kt, qlo=qlo, nkt=nkt: e.matmul(
                                p_all[0:65, 4 + h, qlo:512],
                                lhsT=s_v[:, kt, 2 * hp + h, :],
                                rhs=s_pt[:, par, h, qlo:512],
                                start=(kt == 0), stop=(kt == nkt - 1))
                            if h == 1:
                                vMM_att[(hp, qb, kt)] = pr.inc("tensor", "MM", mk)
                            else:
                                pr.op("tensor", mk)
                    vlast = vMM_att[(hp, qb, nkt - 1)]
                    for h in range(2):
                        pr.wait("vector", "MM", vlast)
                        vr = pr.inc("vector", "VE", lambda e, h=h: e.reciprocal(
                            out=s_rcp[0:1, :], in_=p_all[64:65, 4 + h, 0:512]))
                        pr.wait("tensor", "VE", vr)
                        vb = pr.inc("tensor", "MM", lambda e, h=h: e.matmul(
                            p_all[0:64, 6 + h, 0:512], lhsT=s_one[0:1, 0:64],
                            rhs=s_rcp[0:1, :], start=True, stop=True))
                        pr.wait("scalar", "MM", vb)
                        vrb = pr.inc("scalar", "SC", lambda e, h=h: e.activation(
                            out=s_rb[:], in_=p_all[0:64, 6 + h, 0:512], func=AF.Copy))
                        pr.wait("vector", "SC", vrb)
                        vVE_on[(hp, qb)] = pr.inc("vector", "VE",
                            lambda e, hp=hp, h=h, q0=q0: e.tensor_tensor(
                                out=s_qf[64 * h:64 * h + 64, hp, q0:q0 + 512],
                                in0=p_all[0:64, 4 + h, 0:512], in1=s_rb[:],
                                op=ALU.mult))

            vMM_p2 = pr.ctr["MM"]
            shr = sh_o.rearrange("(r p) t -> r p t", p=128)
            pr.wait("sync", "VE", pr.ctr["VE"])
            vSX_shw = 0
            for c in range(4):
                def mk_shw(e, c=c):
                    rank = e.partition_id()
                    off = (rank % 2) * 4 + c
                    return e.dma_start(out=shr[bass.ts(off, 1)].squeeze(0),
                                       in_=s_qf[:, c, :])
                vSX_shw = pr.inc("sync", "SXO", mk_shw, 16)
            pr.wait("gpsimd", "SXO", vSX_shw)
            pr.inc("gpsimd", "SWF", lambda e: e.dma_start(out=flag_i, in_=s_one[0:1, 0:2]), 16)
            pr.wait("gpsimd", "SWF", None)
            vGP_fl = pr.inc("gpsimd", "GP", lambda e: e.collective_compute(
                "AllGather", ALU.bypass,
                replica_groups=[[0, 1], [2, 3], [4, 5], [6, 7]],
                ins=[flag_i.opt()], outs=[flag_o.opt()]))

        with ExitStack() as es3:
            sb3 = lambda n, s, d=F32, **kw: es3.enter_context(nc.sbuf_tensor(n, s, d, **kw))
            s_out = sb3("s_out", [128, 2, D])
            s_x1 = sb3("s_x1", [128, NOT, D])
            s_hsm = sb3("s_hsm", [128, 2, D], BF16)
            s_hsf = sb3("s_hsf", [128, 8, OWN], BF16, is_transpose_dest=True)
            s_lgt = sb3("s_lgt", [128, NOT, E])
            s_gl = sb3("s_gl", [128, GT, E])
            s_rmx = sb3("s_rmx", [128, GT, 1])
            s_msk = sb3("s_msk", [128, GT, E])
            s_zz = sb3("s_zz", [128, GT, 1])
            s_gate = sb3("s_gate", [128, GT])
            s_tot = sb3("s_tot", [GT, E])
            s_off = sb3("s_off", [GT, E])
            s_offr = sb3("s_offr", [1, GT * E])
            s_gm = sb3("s_gm", [128, NOT])

            pr.wait("sync", "GP", vGP_fl)
            es4 = ExitStack()
            s_oa = es4.enter_context(nc.sbuf_tensor("s_oa", [128, 8, OWN], BF16))

            def mk_shr(e):
                rank = e.partition_id()
                src = sh_o.rearrange("(c p) t -> p c t", p=128)
                return e.dma_start(out=s_oa[:],
                                   in_=src[:, :, bass.ts(rank % 2, OWN)])
            vSX_oa = pr.inc("sync", "SXA", mk_shr, 16)

            xo = x_own.rearrange("(n p) d -> n p d", p=128)
            vVE_x1 = [0] * NOT
            vVE_lge = [0] * NOT
            vSC_hs = [0] * NOT
            vSX_hsf = [0] * NOT
            vMM_lgt = [0] * NOT
            for tt in range(NOT):
                par = tt % 2
                if tt >= 2:
                    pr.wait("sync", "SC", vSC_hs[tt - 2])
                vSX_xo = pr.inc("sync", "SXX%d" % par, lambda e, tt=tt, par=par: e.dma_start(
                    out=s_x[:, par], in_=xo[tt]), 16)
                pr.wait("tensor", "SXA", vSX_oa)
                if tt == 0:
                    pr.wait("tensor", "SWC", v_wo)
                if tt >= 2:
                    pr.wait("tensor", "VE", vVE_x1[tt - 2])
                vMM_op = 0
                for ho in range(2):
                    b = 2 * par + ho
                    for c in range(8):
                        mk = lambda e, c=c, tt=tt, ho=ho, b=b: e.matmul(
                            bank(b), lhsT=s_oa[:, c, 128 * tt:128 * (tt + 1)],
                            rhs=s_wo[:, c, 512 * ho:512 * (ho + 1)],
                            start=(c == 0), stop=(c == 7))
                        if ho == 1 and c == 7:
                            vMM_op = pr.inc("tensor", "MM", mk)
                        else:
                            pr.op("tensor", mk)
                pr.wait("vector", "MM", vMM_op)
                pr.wait("vector", "SXX%d" % par, vSX_xo)
                for ho in range(2):
                    b = 2 * par + ho
                    mk = lambda e, tt=tt, ho=ho, b=b, par=par: e.tensor_tensor(
                        out=s_x1[:, tt, 512 * ho:512 * (ho + 1)],
                        in0=bank(b), in1=s_x[:, par, 512 * ho:512 * (ho + 1)],
                        op=ALU.add)
                    if ho == 1:
                        vVE_x1[tt] = pr.inc("vector", "VE", mk)
                    else:
                        pr.op("vector", mk)

                pr.wait("vector", "VE", vVE_x1[tt])
                v1 = pr.inc("vector", "VE", lambda e, tt=tt, par=par: e.reduce_sum(
                    out=s_st[:, par, 0:1], in_=s_x1[:, tt, :], axis=AX.X, negate=True))
                pr.wait("vector", "VE", v1)
                v2 = pr.inc("vector", "VE", lambda e, par=par: e.tensor_scalar_mul(
                    out=s_st[:, par, 1:2], in0=s_st[:, par, 0:1], scalar1=1.0 / D))
                pr.wait("vector", "VE", v2)
                v_xm = pr.inc("vector", "VE", lambda e, tt=tt, par=par: e.tensor_scalar(
                    out=s_x[:, par], in0=s_x1[:, tt, :], scalar1=s_st[:, par, 1:2],
                    scalar2=None, op0=ALU.add))
                pr.wait("scalar", "VE", v_xm)
                s1 = pr.inc("scalar", "SC", lambda e, par=par: e.activation(
                    out=s_out[:, 0, :], in_=s_x[:, par], func=AF.Square,
                    accum_out=s_st[:, par, 2:3]))
                pr.wait("scalar", "SC", s1)
                s2 = pr.inc("scalar", "SC", lambda e, par=par: e.activation(
                    out=s_st[:, par, 3:4], in_=s_st[:, par, 2:3], func=AF.Ln,
                    scale=cc(C_R1024), bias=cc(C_EPS)))
                pr.wait("scalar", "SC", s2)
                s3 = pr.inc("scalar", "SC", lambda e, par=par: e.activation(
                    out=s_st[:, par, 4:5], in_=s_st[:, par, 3:4], func=AF.Exp,
                    scale=cc(C_NHALF)))
                pr.wait("scalar", "SC", s3)
                if tt >= 2:
                    pr.wait("scalar", "STRL%d" % par, vSX_hsf[tt - 2])
                vSC_hs[tt] = pr.inc("scalar", "SC", lambda e, par=par: e.activation(
                    out=s_hsm[:, par], in_=s_x[:, par], func=AF.Copy,
                    scale=s_st[:, par, 4:5]))
                pr.wait("scalar", "SC", vSC_hs[tt])
                vSX_hsf[tt] = pr.inc("scalar", "STRL%d" % par, lambda e, tt=tt, par=par: e.dma_start(
                    out=s_hsf[:, :, 128 * tt:128 * (tt + 1)], in_=s_hsm[:, par],
                    transpose=True), 16)
                pr.wait("tensor", "STRL%d" % par, vSX_hsf[tt])
                if tt >= 2:
                    pr.wait("tensor", "VE", vVE_lge[tt - 2])
                for c in range(8):
                    mk = lambda e, c=c, tt=tt, par=par: e.matmul(
                        p_all[:, 4 + par, 0:E], lhsT=s_hsf[:, c, 128 * tt:128 * (tt + 1)],
                        rhs=s_wg[:, c, :], start=(c == 0), stop=(c == 7))
                    if c == 7:
                        vMM_lgt[tt] = pr.inc("tensor", "MM", mk)
                    else:
                        pr.op("tensor", mk)
                pr.wait("vector", "MM", vMM_lgt[tt])
                vVE_lge[tt] = pr.inc("vector", "VE", lambda e, tt=tt, par=par: e.tensor_copy(
                    out=s_lgt[:, tt, :], in_=p_all[:, 4 + par, 0:E]))

            es4.close()
            s_we = sb3("s_we", [128, 8, D], BF16)
            pr.wait("gpsimd", "MM", vMM_op)
            v_wall = g_dma(s_we[:], we_i.rearrange("(c p) j -> p c j", p=128), "SWD")

            pr.wait("sync", "VE", pr.ctr["VE"])
            vSX_lg = pr.inc("sync", "SXL", lambda e: e.dma_start(
                out=lg_i.rearrange("(n p) e -> p n e", p=128), in_=s_lgt[:]), 16)
            pr.wait("gpsimd", "SXL", vSX_lg)
            vGP_lg = pr.inc("gpsimd", "GP", lambda e: e.collective_compute(
                "AllGather", ALU.bypass, replica_groups=[[0, 1, 2, 3, 4, 5, 6, 7]],
                ins=[lg_i.opt()], outs=[lg_o.opt()]))
            pr.wait("sync", "GP", vGP_lg)
            vSX_gl = pr.inc("sync", "SXG", lambda e: e.dma_start(
                out=s_gl[:], in_=lg_o.rearrange("(g p) e -> p g e", p=128)), 16)

            pr.wait("vector", "SXG", vSX_gl)
            v1 = pr.inc("vector", "VE", lambda e: e.reduce_max(
                out=s_rmx[:], in_=s_gl[:], axis=AX.X))
            pr.wait("vector", "VE", v1)
            rbc = s_rmx[:].to_broadcast((128, GT, E))
            vmsk = pr.inc("vector", "VE", lambda e: e.tensor_tensor(
                out=s_msk[:], in0=s_gl[:], in1=rbc, op=ALU.is_equal))
            pr.wait("vector", "VE", vmsk)
            v2 = pr.inc("vector", "VE", lambda e: e.tensor_tensor(
                out=s_gl[:], in0=s_gl[:], in1=rbc, op=ALU.subtract))
            pr.wait("scalar", "VE", v2)
            s1 = pr.inc("scalar", "SC", lambda e: e.activation(
                out=s_gl[:], in_=s_gl[:], func=AF.Exp))
            pr.wait("vector", "SC", s1)
            v3 = pr.inc("vector", "VE", lambda e: e.reduce_sum(
                out=s_zz[:], in_=s_gl[:], axis=AX.X))
            pr.wait("vector", "VE", v3)
            v4 = pr.inc("vector", "VE", lambda e: e.reciprocal(
                out=s_gate[:].unsqueeze(2), in_=s_zz[:]))
            pv_rank = p_all[:, 0:2, :].rearrange("p b f -> p (b f)")[:, 0:GT * E].rearrange(
                "p (g e) -> p g e", e=E)
            msk_flat = s_msk[:].rearrange("p g e -> p (g e)")
            pr.wait("tensor", "MM", vMM_p2)
            pr.wait("tensor", "VE", pr.ctr["VE"])
            n_tchunk = (GT * E + 511) // 512
            vMM_tot = 0
            for j in range(n_tchunk):
                lo, hi = 512 * j, min(512 * (j + 1), GT * E)
                vMM_tot = pr.inc("tensor", "MM", lambda e, j=j, lo=lo, hi=hi: e.matmul(
                    p_all[0:1, 2 + j, 0:hi - lo], lhsT=s_tri[:, 0, 127:128],
                    rhs=msk_flat[:, lo:hi], start=True, stop=True))
            pr.wait("scalar", "MM", vMM_tot)
            s_tote = pr.inc("scalar", "SC", lambda e: e.activation(
                out=s_offr[0:1, :],
                in_=p_all[0:1, 2:2 + n_tchunk, :].rearrange(
                    "p b f -> p (b f)")[:, 0:GT * E], func=AF.Copy))
            pr.wait("sync", "SC", s_tote)
            v6 = pr.inc("sync", "SXR", lambda e: e.dma_start(
                out=scr.rearrange("g e -> (g e)").unsqueeze(0), in_=s_offr[0:1, :]), 16)
            pr.wait("sync", "SXR", v6)
            v7 = pr.inc("sync", "SXR", lambda e: e.dma_start(out=s_tot[:], in_=scr), 16)
            pr.wait("tensor", "SXR", v7)
            pr.wait("tensor", "SC", s_tote)
            vMM_of = pr.inc("tensor", "MM", lambda e: e.matmul(
                p_all[0:GT, 2, 0:E], lhsT=s_tri[0:GT, 1, 0:GT], rhs=s_tot[:],
                start=True, stop=True))
            pr.wait("scalar", "MM", vMM_of)
            s2 = pr.inc("scalar", "SC", lambda e: e.activation(
                out=s_off[:], in_=p_all[0:GT, 2, 0:E], func=AF.Copy))
            pr.wait("sync", "SC", s2)
            v8 = pr.inc("sync", "SXR", lambda e: e.dma_start(out=scr2, in_=s_off[:]), 16)
            pr.wait("sync", "SXR", v8)
            v9 = pr.inc("sync", "SXR", lambda e: e.dma_start(
                out=s_offr[:], in_=scr2.rearrange("g e -> (g e)").unsqueeze(0)), 16)
            pr.wait("tensor", "SXR", v9)
            vMM_rk2 = 0
            for g in range(GT):
                pr.op("tensor", lambda e, g=g: e.matmul(
                    pv_rank[:, g, :], lhsT=s_tri[:, 0], rhs=s_msk[:, g, :],
                    start=True, stop=False))
                vMM_rk2 = pr.inc("tensor", "MM", lambda e, g=g: e.matmul(
                    pv_rank[:, g, :], lhsT=s_one[:], rhs=s_offr[0:1, E * g:E * (g + 1)],
                    start=False, stop=True))
            pr.wait("vector", "MM", vMM_rk2)
            v9b = pr.inc("vector", "VE", lambda e: e.tensor_scalar(
                out=s_gl[:], in0=pv_rank, scalar1=cc(C_CAP), scalar2=None,
                op0=ALU.is_le))
            pr.wait("vector", "VE", v9b)
            v10 = pr.inc("vector", "VE", lambda e: e.tensor_tensor(
                out=s_gl[:], in0=s_gl[:], in1=s_msk[:], op=ALU.mult))
            pr.wait("vector", "VE", v10)
            v11 = pr.inc("vector", "VE", lambda e: e.reduce_sum(
                out=s_zz[:], in_=s_gl[:], axis=AX.X))
            pr.wait("vector", "VE", v11)
            v12 = pr.inc("vector", "VE", lambda e: e.tensor_tensor(
                out=s_gate[:], in0=s_gate[:], in1=s_zz[:].squeeze(2), op=ALU.mult))
            pr.wait("sync", "VE", v12)
            v13 = pr.inc("sync", "SXR", lambda e: e.dma_start(out=gate_d, in_=s_gate[:]), 16)
            pr.wait("sync", "SXR", v13)

            def mk_gm(e):
                rank = e.partition_id()
                return e.dma_start(out=s_gm[:], in_=gate_d[:, bass.ts(rank, NOT)])
            vSX_gm = pr.inc("sync", "SXR", mk_gm, 16)

            oo = out_o.rearrange("(n p) d -> n p d", p=128)
            for tt in range(NOT):
                par = tt % 2
                if tt == 0:
                    pr.wait("tensor", "MM", vMM_rk2)
                    pr.wait("tensor", "VE", pr.ctr["VE"])
                    pr.wait("tensor", "SWD", v_wall)
                if tt >= 2:
                    pr.wait("tensor", "VE", vVE_x1[tt - 2])
                vMM_y = 0
                for ho in range(2):
                    b = 2 * par + ho
                    for c in range(8):
                        pr.op("tensor", lambda e, c=c, tt=tt, ho=ho, b=b: e.matmul(
                            bank(b), lhsT=s_hsf[:, c, 128 * tt:128 * (tt + 1)],
                            rhs=s_we[:, c, 512 * ho:512 * (ho + 1)],
                            start=(c == 0), stop=False))
                    mkb = lambda e, ho=ho, b=b: e.matmul(
                        bank(b), lhsT=s_one[:], rhs=s_be[0:1, 512 * ho:512 * (ho + 1)],
                        start=False, stop=True)
                    if ho == 1:
                        vMM_y = pr.inc("tensor", "MM", mkb)
                    else:
                        pr.op("tensor", mkb)
                pr.wait("vector", "MM", vMM_y)
                pr.wait("vector", "SXR", vSX_gm)
                if tt >= 2:
                    pr.wait("vector", "ST%d" % par, 16 * (tt // 2))
                for ho in range(2):
                    b = 2 * par + ho
                    mk = lambda e, tt=tt, ho=ho, b=b, par=par: e.scalar_tensor_tensor(
                        out=s_out[:, par, 512 * ho:512 * (ho + 1)], in0=bank(b),
                        scalar=s_gm[:, tt:tt + 1],
                        in1=s_x1[:, tt, 512 * ho:512 * (ho + 1)],
                        op0=ALU.mult, op1=ALU.add)
                    if ho == 1:
                        vVE_x1[tt] = pr.inc("vector", "VE", mk)
                    else:
                        pr.op("vector", mk)
                pr.wait("sync", "VE", vVE_x1[tt])
                pr.inc("sync", "ST%d" % par, lambda e, tt=tt, par=par: e.dma_start(
                    out=oo[tt], in_=s_out[:, par]), 16)

            pr.wait("sync", "ST0", None)
            pr.wait("sync", "ST1", None)

        with nc.Block() as blk:
            pr.emit(blk)

    return nc


def host_tables(T):
    pos = np.arange(T, dtype=np.float64)
    i = np.arange(0, DH, 2, dtype=np.float64) / DH
    theta = pos[:, None] / (ROPE_BASE ** i[None, :])
    twopi = 2 * np.pi

    def red(th):
        r = np.mod(th + np.pi, twopi) - np.pi
        hi = r.astype(np.float32)
        lo = (r - hi.astype(np.float64)).astype(np.float32)
        return hi, lo

    a, b = red(theta)
    c, d = red(theta + np.pi / 2)
    return np.stack([a, b, c, d]).astype(np.float32)


def rope_perm():
    p = np.arange(512).reshape(8, 64)
    q = np.concatenate([p[:, 0::2], p[:, 1::2]], axis=1)
    return q.reshape(-1)


def make_inputs(full, T):
    (query, wq, wk, wv, wo, Aq, Bq, Av, Bv, w_gate, w_expert, b_expert) = full
    perm = rope_perm()
    th = host_tables(T)
    tri_i = np.triu(np.ones((128, 128), np.float32), 0)
    trs_i = np.triu(np.ones((128, 128), np.float32), 1)
    nid_i = (np.eye(128, dtype=np.float32) * -1e9)
    one_i = np.ones((1, 128), np.float32)
    cst = np.zeros((128, 16), np.float32)
    cst[:, C_EPS] = 1e-5
    cst[:, C_R1024] = 1.0 / D
    cst[:, C_NHALF] = -0.5
    cst[:, C_R8] = 0.125
    cst[:, C_SHIFT] = -12.0
    cst[:, C_CAP] = 100.5
    maps = []
    for c in range(8):
        b, hh = c // 2, c % 2
        sl = slice(512 * hh, 512 * (hh + 1))
        wq_s = np.ascontiguousarray(wq[:, sl][:, perm])
        wk_s = np.ascontiguousarray(wk[:, sl][:, perm])
        wv_s = np.ascontiguousarray(wv[:, sl])
        bq_s = np.ascontiguousarray(Bq[:, sl][:, perm])
        bv_s = np.ascontiguousarray(Bv[:, sl])
        xb = np.ascontiguousarray(query[b, :T])
        maps.append(dict(
            x_in=xb, x_own=np.ascontiguousarray(xb[hh * (T // 2):(hh + 1) * (T // 2)]),
            wq=wq_s, wk=wk_s, wv=wv_s, wo_i=np.ascontiguousarray(wo),
            aq=np.ascontiguousarray(Aq), av=np.ascontiguousarray(Av),
            bq_i=bq_s, bv_i=bv_s, wg_i=np.ascontiguousarray(w_gate),
            be_i=np.ascontiguousarray(b_expert.reshape(1, D)),
            we_i=np.ascontiguousarray(w_expert),
            th_i=np.ascontiguousarray(
                th[:, :T].reshape(4, T // 128, 128, 32).transpose(2, 0, 1, 3).reshape(128, -1)),
            tri_i=tri_i, trs_i=trs_i, nid_i=nid_i,
            one_i=one_i, cst_i=cst,
        ))
    return maps


def kernel(L0_attn_query, L0_attn_wq, L0_attn_wk, L0_attn_wv, L0_attn_wo,
           L0_attn_Aq, L0_attn_Bq, L0_attn_Av, L0_attn_Bv,
           L0_ffn_w_gate, L0_ffn_w_expert, L0_ffn_b_expert):
    T = L0_attn_query.shape[1]
    full = (np.asarray(L0_attn_query, np.float32), np.asarray(L0_attn_wq, np.float32),
            np.asarray(L0_attn_wk, np.float32), np.asarray(L0_attn_wv, np.float32),
            np.asarray(L0_attn_wo, np.float32), np.asarray(L0_attn_Aq, np.float32),
            np.asarray(L0_attn_Bq, np.float32), np.asarray(L0_attn_Av, np.float32),
            np.asarray(L0_attn_Bv, np.float32), np.asarray(L0_ffn_w_gate, np.float32),
            np.asarray(L0_ffn_w_expert, np.float32),
            np.asarray(L0_ffn_b_expert, np.float32))
    maps = make_inputs(full, T)
    nc = build_nc(T)
    res = run_bass_kernel_spmd(nc, maps, core_ids=list(range(8)))
    out = np.empty((B, T, D), np.float32)
    for c in range(8):
        b, hh = c // 2, c % 2
        out[b, hh * (T // 2):(hh + 1) * (T // 2)] = res.results[c]["out_o"]
    return out


# version 09
# speedup vs baseline: 1.4091x; 1.4091x over previous
import numpy as np
import concourse.bass as bass
import concourse.mybir as mybir
from concourse.bass_utils import run_bass_kernel_spmd

F32 = mybir.dt.float32
BF16 = mybir.dt.bfloat16
AF = mybir.ActivationFunctionType
ALU = mybir.AluOpType
AX = mybir.AxisListType

B, D = 4, 1024
NH, DH = 16, 64
E = 8
ROPE_BASE = 10000.0

ENGS = ("sync", "gpsimd", "tensor", "vector", "scalar")
C_EPS, C_R1024, C_NHALF, C_R8, C_SHIFT, C_CAP, C_NONE = 0, 1, 2, 3, 4, 5, 6


class Prog:

    def __init__(self, nc, es):
        self.nc = nc
        self.q = {e: [] for e in ENGS}
        self.ctr = {}
        self.h = {}
        self.es = es

    def sem(self, name):
        self.h[name] = self.es.enter_context(self.nc.semaphore(name))
        self.ctr[name] = 0

    def add(self, eng, fn):
        self.q[eng].append(fn)

    def inc(self, eng, name, mk, k=1):
        self.ctr[name] += k
        h, kk = self.h[name], k
        self.add(eng, lambda e: mk(e).then_inc(h, kk))
        return self.ctr[name]

    def op(self, eng, mk):
        self.add(eng, lambda e: mk(e))

    def wait(self, eng, name, v=None):
        h = self.h[name]
        v = self.ctr[name] if v is None else v
        assert v >= 0
        self.q[eng].append(lambda e, h=h, v=v: e.wait_ge(h, v))
        return v

    def emit(self, block):
        for eng in ENGS:
            ops = self.q[eng]
            getattr(block, eng)(lambda e, ops=ops: [f(e) for f in ops])


def build_nc(T):
    TOK = T
    OWN = T // 2
    NT = TOK // 128
    NOT = OWN // 128
    GT = (4 * T) // 128
    QB = 512
    NQB = TOK // QB
    NF = TOK // 128
    assert GT <= 128

    nc = bass.Bass("TRN2", target_bir_lowering=False)
    dt = lambda n, s, d=F32, **kw: nc.dram_tensor(n, s, d, **kw).ap()

    x_in = dt("x_in", [TOK, D], kind="ExternalInput")
    x_own = dt("x_own", [OWN, D], kind="ExternalInput")
    wq = dt("wq", [D, 512], kind="ExternalInput")
    wk = dt("wk", [D, 512], kind="ExternalInput")
    wv = dt("wv", [D, 512], kind="ExternalInput")
    wo_i = dt("wo_i", [D, D], kind="ExternalInput")
    aq = dt("aq", [D, E], kind="ExternalInput")
    av = dt("av", [D, E], kind="ExternalInput")
    bq_i = dt("bq_i", [E, 512], kind="ExternalInput")
    bv_i = dt("bv_i", [E, 512], kind="ExternalInput")
    wg_i = dt("wg_i", [D, E], kind="ExternalInput")
    be_i = dt("be_i", [1, D], kind="ExternalInput")
    we_i = dt("we_i", [D, D], kind="ExternalInput")
    th_i = dt("th_i", [128, 4 * (TOK // 128) * 32], kind="ExternalInput")
    tri_i = dt("tri_i", [128, 128], kind="ExternalInput")
    trs_i = dt("trs_i", [128, 128], kind="ExternalInput")
    nid_i = dt("nid_i", [128, 128], kind="ExternalInput")
    one_i = dt("one_i", [1, 128], kind="ExternalInput")
    cst_i = dt("cst_i", [128, 16], kind="ExternalInput")
    out_o = dt("out_o", [OWN, D], kind="ExternalOutput")

    sh_o = dt("sh_o", [D, TOK], BF16, kind="Internal", addr_space="Shared")
    flag_i = dt("flag_i", [1, 2], kind="Internal")
    flag_o = dt("flag_o", [2, 2], kind="Internal")
    lg_i = dt("lg_i", [OWN, E], kind="Internal")
    lg_o = dt("lg_o", [4 * T, E], kind="Internal")
    scr = dt("scr", [GT, E], kind="Internal")
    scr2 = dt("scr2", [GT, E], kind="Internal")
    gate_d = dt("gate_d", [128, GT], kind="Internal")

    from contextlib import ExitStack
    with ExitStack() as es0:
        sb0 = lambda n, s, d=F32, **kw: es0.enter_context(nc.sbuf_tensor(n, s, d, **kw))

        s_cst = sb0("s_cst", [128, 16])
        s_one = sb0("s_one", [1, 128])
        s_tri = sb0("s_tri", [128, 2, 128])
        s_trs = sb0("s_trs", [128, 128], BF16)
        s_nid = sb0("s_nid", [128, 128], BF16)
        s_x = sb0("s_x", [128, 2, D])
        s_st = sb0("s_st", [128, 2, 8])
        s_wo = sb0("s_wo", [128, 8, D], BF16)
        s_wg = sb0("s_wg", [128, 8, E], BF16)
        s_be = sb0("s_be", [1, D])
        s_sc = sb0("s_sc", [128, 2, NF, 32])

        p_all = es0.enter_context(nc.psum_tensor("p_all", [128, 8, 512], F32))
        bank = lambda i, lo=0, hi=512: p_all[:, i, lo:hi]

        pr = Prog(nc, es0)
        for s in ("SWA", "SWB", "SWC", "SWF", "SXX0", "SXX1", "STRL0", "STRL1",
                  "SXQ0", "SXQ1", "SXO", "SXA", "SXL", "SXG", "SXR", "SWD",
                  "ST0", "ST1", "MM", "VE", "SC", "GP"):
            pr.sem(s)
        cc = lambda i: s_cst[:, i:i + 1]

        def g_dma(dst, src, sem="SWA"):
            return pr.inc("gpsimd", sem,
                          lambda e, d=dst, s=src: e.dma_start(out=d, in_=s), 16)

        g_dma(s_cst[:], cst_i)
        g_dma(s_tri[:, 0], tri_i)
        g_dma(s_tri[:, 1], trs_i)
        g_dma(s_trs[:], trs_i)
        g_dma(s_nid[:], nid_i)
        g_dma(s_one[:], one_i)
        g_dma(s_wg[:], wg_i.rearrange("(c p) j -> p c j", p=128))
        g_dma(s_be[:], be_i)

        with ExitStack() as es1:
            sb1 = lambda n, s, d=F32, **kw: es1.enter_context(nc.sbuf_tensor(n, s, d, **kw))
            s_th = sb1("s_th", [128, 4, NF, 32])
            s_tm = sb1("s_tm", [128, 2, NF, 32])
            v_th = g_dma(s_th[:].rearrange("p k n f -> p (k n f)"), th_i)

            pr.wait("scalar", "SWA", v_th)
            pr.inc("scalar", "SC", lambda e: e.activation(
                out=s_sc[:, 0], in_=s_th[:, 0], func=AF.Sin))
            v_raw = pr.inc("scalar", "SC", lambda e: e.activation(
                out=s_sc[:, 1], in_=s_th[:, 2], func=AF.Sin))
            pr.wait("vector", "SC", v_raw)
            pr.op("vector", lambda e: e.tensor_tensor(
                out=s_tm[:, 0], in0=s_th[:, 1], in1=s_sc[:, 1], op=ALU.mult))
            v_m = pr.inc("vector", "VE", lambda e: e.tensor_tensor(
                out=s_tm[:, 1], in0=s_th[:, 3], in1=s_sc[:, 0], op=ALU.mult))
            pr.wait("vector", "VE", v_m)
            pr.op("vector", lambda e: e.tensor_tensor(
                out=s_sc[:, 0], in0=s_sc[:, 0], in1=s_tm[:, 0], op=ALU.add))
            v_tab = pr.inc("vector", "VE", lambda e: e.tensor_tensor(
                out=s_sc[:, 1], in0=s_sc[:, 1], in1=s_tm[:, 1], op=ALU.subtract))

        with ExitStack() as es2:
            sb2 = lambda n, s, d=F32, **kw: es2.enter_context(nc.sbuf_tensor(n, s, d, **kw))
            s_sq = sb2("s_sq", [128, D])
            s_ln = sb2("s_ln", [128, 2, D], BF16)
            s_lnf = sb2("s_lnf", [128, 2, 8, 128], BF16, is_transpose_dest=True)
            s_wq = sb2("s_wq", [128, 8, 512], BF16)
            s_wk = sb2("s_wk", [128, 8, 512], BF16)
            s_wv = sb2("s_wv", [128, 8, 512], BF16)
            s_aqv = sb2("s_aqv", [128, 8, 40], BF16)
            s_bq = sb2("s_bq", [E, 512])
            s_bv = sb2("s_bv", [E, 512])
            s_b4 = sb2("s_b4", [64, 512], BF16)
            s_t1 = sb2("s_t1", [64, 2, 128], BF16)
            s_rm = sb2("s_rm", [128, 4, 8, 32])
            s_qtm = sb2("s_qtm", [128, 2, 512], BF16)
            s_ktm = sb2("s_ktm", [128, 2, 512], BF16)
            s_qf = sb2("s_qf", [128, 4, TOK], BF16, is_transpose_dest=True)
            s_kf = sb2("s_kf", [128, 4, TOK], BF16, is_transpose_dest=True)
            s_v = sb2("s_v", [128, NT, 8, 65], BF16)
            s_pt = sb2("s_pt", [128, 2, 2, 512], BF16)
            s_rcp = sb2("s_rcp", [1, 512])
            s_rb = sb2("s_rb", [64, 512])

            pr.wait("gpsimd", "VE", v_tab)
            g_dma(s_wq[:], wq.rearrange("(c p) j -> p c j", p=128), "SWB")
            g_dma(s_wk[:], wk.rearrange("(c p) j -> p c j", p=128), "SWB")
            g_dma(s_wv[:], wv.rearrange("(c p) j -> p c j", p=128), "SWB")
            g_dma(s_aqv[:, :, 0:8], aq.rearrange("(c p) r -> p c r", p=128), "SWB")
            g_dma(s_aqv[:, :, 32:40], av.rearrange("(c p) r -> p c r", p=128), "SWB")
            g_dma(s_bq[:], bq_i, "SWB")
            v_w1 = g_dma(s_bv[:], bv_i, "SWB")
            v_wo = g_dma(s_wo[:], wo_i.rearrange("(c p) j -> p c j", p=128), "SWC")

            pr.wait("vector", "SWB", v_w1)
            pr.op("vector", lambda e: e.memset(s_aqv[:, :, 8:32], 0.0))
            pr.op("vector", lambda e: e.tensor_scalar_mul(
                out=s_b4[0:8, :], in0=s_bq[:], scalar1=4.0))
            pr.op("vector", lambda e: e.tensor_scalar_mul(
                out=s_b4[32:40, :], in0=s_bv[:], scalar1=4.0))
            v_vone = pr.inc("vector", "VE", lambda e: e.memset(s_v[:, :, :, 64:65], 1.0))

            xr = x_in.rearrange("(n p) d -> n p d", p=128)
            vSC_ln = [0] * NT
            vSC_vev = [0] * NT
            vVE_rope = [0] * NT
            vSX_lnf = [0] * NT
            vSX_qk = [0] * NT
            vMM_qkv = [0] * NT
            vMM_t1 = [0] * NT
            vVE_t1e = [0] * NT

            for i in range(NT):
                par = i % 2
                bq_b, bk_b, bv_b, bt_b = 4 * par + 0, 4 * par + 1, 4 * par + 2, 4 * par + 3

                if i >= 2:
                    pr.wait("sync", "SC", vSC_ln[i - 2])
                vSX_x = pr.inc("sync", "SXX%d" % par,
                               lambda e, i=i, par=par: e.dma_start(
                                   out=s_x[:, par], in_=xr[i]), 16)

                pr.wait("vector", "SXX%d" % par, vSX_x)
                v1 = pr.inc("vector", "VE", lambda e, par=par: e.reduce_sum(
                    out=s_st[:, par, 0:1], in_=s_x[:, par], axis=AX.X, negate=True))
                pr.wait("vector", "VE", v1)
                v2 = pr.inc("vector", "VE", lambda e, par=par: e.tensor_scalar_mul(
                    out=s_st[:, par, 1:2], in0=s_st[:, par, 0:1], scalar1=1.0 / D))
                pr.wait("vector", "VE", v2)
                v_xm = pr.inc("vector", "VE", lambda e, par=par: e.tensor_scalar_add(
                    out=s_x[:, par], in0=s_x[:, par], scalar1=s_st[:, par, 1:2]))
                pr.wait("scalar", "VE", v_xm)
                s1 = pr.inc("scalar", "SC", lambda e, par=par: e.activation(
                    out=s_sq[:], in_=s_x[:, par], func=AF.Square,
                    accum_out=s_st[:, par, 2:3]))
                pr.wait("scalar", "SC", s1)
                s2 = pr.inc("scalar", "SC", lambda e, par=par: e.activation(
                    out=s_st[:, par, 3:4], in_=s_st[:, par, 2:3], func=AF.Ln,
                    scale=cc(C_R1024), bias=cc(C_EPS)))
                pr.wait("scalar", "SC", s2)
                s3 = pr.inc("scalar", "SC", lambda e, par=par: e.activation(
                    out=s_st[:, par, 4:5], in_=s_st[:, par, 3:4], func=AF.Exp,
                    scale=cc(C_NHALF)))
                pr.wait("scalar", "SC", s3)
                if i >= 2:
                    pr.wait("scalar", "STRL%d" % par, vSX_lnf[i - 2])
                vSC_ln[i] = pr.inc("scalar", "SC", lambda e, par=par: e.activation(
                    out=s_ln[:, par], in_=s_x[:, par], func=AF.Copy,
                    scale=s_st[:, par, 4:5]))

                pr.wait("scalar", "SC", vSC_ln[i])
                if i >= 2:
                    pr.wait("scalar", "MM", vMM_qkv[i - 2])
                vSX_lnf[i] = pr.inc("scalar", "STRL%d" % par, lambda e, par=par: e.dma_start(
                    out=s_lnf[:, par], in_=s_ln[:, par], transpose=True), 16)

                pr.wait("tensor", "STRL%d" % par, vSX_lnf[i])
                if i == 0:
                    pr.wait("tensor", "SWB", v_w1)
                    pr.wait("tensor", "SWA", None)
                    pr.wait("tensor", "VE", v_vone)
                if i >= 2:
                    pr.wait("tensor", "VE", vVE_rope[i - 2])
                    pr.wait("tensor", "SC", vSC_vev[i - 2])
                    pr.wait("tensor", "VE", vVE_t1e[i - 2])
                for c in range(7):
                    pr.op("tensor", lambda e, c=c, par=par, bt=bt_b: e.matmul(
                        p_all[0:40, bt, 0:128], lhsT=s_aqv[:, c, :],
                        rhs=s_lnf[:, par, c, :], start=(c == 0), stop=False))
                vMM_t1[i] = pr.inc("tensor", "MM", lambda e, par=par, bt=bt_b: e.matmul(
                    p_all[0:40, bt, 0:128], lhsT=s_aqv[:, 7, :],
                    rhs=s_lnf[:, par, 7, :], start=False, stop=True))
                for c in range(8):
                    pr.op("tensor", lambda e, c=c, par=par, b=bq_b: e.matmul(
                        bank(b), lhsT=s_lnf[:, par, c, :], rhs=s_wq[:, c, :],
                        start=(c == 0), stop=False))
                    pr.op("tensor", lambda e, c=c, par=par, b=bk_b: e.matmul(
                        bank(b), lhsT=s_lnf[:, par, c, :], rhs=s_wk[:, c, :],
                        start=(c == 0), stop=(c == 7)))
                    pr.op("tensor", lambda e, c=c, par=par, b=bv_b: e.matmul(
                        bank(b), lhsT=s_lnf[:, par, c, :], rhs=s_wv[:, c, :],
                        start=(c == 0), stop=False))
                pr.wait("vector", "MM", vMM_t1[i])
                vVE_t1e[i] = pr.inc("vector", "VE", lambda e, par=par, bt=bt_b: e.tensor_copy(
                    out=s_t1[0:40, par, :], in_=p_all[0:40, bt, 0:128]))
                pr.wait("tensor", "VE", vVE_t1e[i])
                pr.op("tensor", lambda e, par=par, b=bq_b: e.matmul(
                    bank(b), lhsT=s_t1[0:8, par, :], rhs=s_b4[0:8, :],
                    start=False, stop=True))
                vMM_qkv[i] = pr.inc("tensor", "MM", lambda e, par=par, b=bv_b: e.matmul(
                    bank(b), lhsT=s_t1[32:40, par, :], rhs=s_b4[32:40, :],
                    start=False, stop=True))

                if i == 0:
                    pr.wait("vector", "VE", v_tab)
                pr.wait("vector", "MM", vMM_qkv[i])
                if i >= 2:
                    pr.wait("vector", "SXQ%d" % par, vSX_qk[i - 2])
                co = s_sc[:, 1, i:i + 1, :].to_broadcast((128, 8, 32))
                si = s_sc[:, 0, i:i + 1, :].to_broadcast((128, 8, 32))
                for (bnk, stm) in ((bq_b, s_qtm), (bk_b, s_ktm)):
                    pv = p_all[:, bnk, :].rearrange("p (h c) -> p h c", h=8)
                    qE, qO = pv[:, :, 0:32], pv[:, :, 32:64]
                    ot = stm[:, par, :].rearrange("p (h c) -> p h c", h=8)
                    pr.op("vector", lambda e, qE=qE, co=co: e.tensor_tensor(
                        out=s_rm[:, 0], in0=qE, in1=co, op=ALU.mult))
                    pr.op("vector", lambda e, qO=qO, si=si: e.tensor_tensor(
                        out=s_rm[:, 1], in0=qO, in1=si, op=ALU.mult))
                    pr.op("vector", lambda e, qE=qE, si=si: e.tensor_tensor(
                        out=s_rm[:, 2], in0=qE, in1=si, op=ALU.mult))
                    v3 = pr.inc("vector", "VE", lambda e, qO=qO, co=co: e.tensor_tensor(
                        out=s_rm[:, 3], in0=qO, in1=co, op=ALU.mult))
                    pr.wait("vector", "VE", v3)
                    pr.op("vector", lambda e, ot=ot: e.tensor_tensor(
                        out=ot[:, :, 0:32], in0=s_rm[:, 0], in1=s_rm[:, 1],
                        op=ALU.subtract))
                    v4 = pr.inc("vector", "VE", lambda e, ot=ot: e.tensor_tensor(
                        out=ot[:, :, 32:64], in0=s_rm[:, 2], in1=s_rm[:, 3],
                        op=ALU.add))
                    pr.wait("vector", "VE", v4)
                vVE_rope[i] = v4
                pr.wait("scalar", "MM", vMM_qkv[i])
                vSC_vev[i] = pr.inc("scalar", "SC", lambda e, par=par, i=i, b=bv_b: e.activation(
                    out=s_v[:, i, :, 0:64],
                    in_=bank(b).rearrange("p (h c) -> p h c", h=8), func=AF.Copy))

                pr.wait("sync", "VE", vVE_rope[i])
                pr.inc("sync", "SXQ%d" % par, lambda e, par=par, i=i: e.dma_start(
                    out=s_qf[:, :, 128 * i:128 * (i + 1)], in_=s_qtm[:, par, :],
                    transpose=True), 16)
                vSX_qk[i] = pr.inc("sync", "SXQ%d" % par, lambda e, par=par, i=i: e.dma_start(
                    out=s_kf[:, :, 128 * i:128 * (i + 1)], in_=s_ktm[:, par, :],
                    transpose=True), 16)

            vSC_p1 = pr.ctr["SC"]

            vSC_exp = {}
            vMM_att = {}
            vVE_on = {}
            vb_prev = [0]
            von_prev = [0]
            pr.wait("tensor", "SXQ0", None)
            pr.wait("tensor", "SXQ1", None)
            pr.wait("tensor", "SC", vSC_p1)
            for hp in range(4):
                for qb in range(NQB):
                    q0 = qb * QB
                    nkt = q0 // 128 + 4
                    def emit_scores_exp(kt):
                        par = kt % 2
                        koff = kt * 128
                        qlo = max(0, koff - q0)
                        if (hp, qb, kt - 2) in vSC_exp:
                            pr.wait("tensor", "SC", vSC_exp[(hp, qb, kt - 2)])
                        elif qb > 0 and kt < 2:
                            k_prev = (q0 - QB) // 128 + 4
                            if (hp, qb - 1, k_prev - 2 + kt) in vSC_exp:
                                pr.wait("tensor", "SC",
                                        vSC_exp[(hp, qb - 1, k_prev - 2 + kt)])
                        elif hp > 0 and qb == 0 and kt < 2:
                            pr.wait("tensor", "SC",
                                    vSC_exp[(hp - 1, NQB - 1, (NQB - 1) * QB // 128 + 2 + kt)])
                        diag = koff >= q0
                        vS = None
                        for h in range(2):
                            mk1 = lambda e, hp=hp, h=h, par=par, koff=koff, \
                                  q0=q0, qlo=qlo, diag=diag: e.matmul(
                                p_all[:, 2 * par + h, qlo:512],
                                lhsT=s_kf[64 * h:64 * h + 64, hp, koff:koff + 128],
                                rhs=s_qf[64 * h:64 * h + 64, hp, q0 + qlo:q0 + 512],
                                start=True, stop=not diag)
                            mk2 = lambda e, h=h, par=par, qlo=qlo: e.matmul(
                                p_all[:, 2 * par + h, qlo:qlo + 128],
                                lhsT=s_trs[:], rhs=s_nid[:],
                                start=False, stop=True)
                            if diag:
                                pr.op("tensor", mk1)
                                if h == 1:
                                    vS = pr.inc("tensor", "MM", mk2)
                                else:
                                    pr.op("tensor", mk2)
                            else:
                                if h == 1:
                                    vS = pr.inc("tensor", "MM", mk1)
                                else:
                                    pr.op("tensor", mk1)
                        pr.wait("scalar", "MM", vS)
                        if (hp, qb, kt - 2) in vMM_att:
                            pr.wait("scalar", "MM", vMM_att[(hp, qb, kt - 2)])
                        vSC_exp[(hp, qb, kt)] = pr.inc("scalar", "SC",
                            lambda e, par=par, qlo=qlo: e.activation(
                                out=s_pt[:, par, :, qlo:512],
                                in_=p_all[:, 2 * par:2 * par + 2, qlo:512],
                                func=AF.Exp, scale=cc(C_R8), bias=cc(C_SHIFT)))

                    def emit_attnv(kt):
                        par = kt % 2
                        koff = kt * 128
                        qlo = max(0, koff - q0)
                        pr.wait("tensor", "SC", vSC_exp[(hp, qb, kt)])
                        if kt == 0 and (hp, qb - 1) in vVE_on:
                            pr.wait("tensor", "VE", vVE_on[(hp, qb - 1)])
                        if kt == 0 and qb == 0 and (hp - 1, NQB - 1) in vVE_on:
                            pr.wait("tensor", "VE", vVE_on[(hp - 1, NQB - 1)])
                        for h in range(2):
                            mk = lambda e, hp=hp, h=h, par=par, kt=kt, qlo=qlo, nkt=nkt: e.matmul(
                                p_all[0:65, 4 + h, qlo:512],
                                lhsT=s_v[:, kt, 2 * hp + h, :],
                                rhs=s_pt[:, par, h, qlo:512],
                                start=(kt == 0), stop=(kt == nkt - 1))
                            if h == 1:
                                vMM_att[(hp, qb, kt)] = pr.inc("tensor", "MM", mk)
                            else:
                                pr.op("tensor", mk)

                    for kt in range(nkt):
                        emit_scores_exp(kt)
                        if kt >= 1:
                            emit_attnv(kt - 1)
                    emit_attnv(nkt - 1)
                    vlast = vMM_att[(hp, qb, nkt - 1)]
                    for h in range(2):
                        pr.wait("scalar", "MM", vlast)
                        if vb_prev[0]:
                            pr.wait("scalar", "MM", vb_prev[0])
                        vln = pr.inc("scalar", "SC", lambda e, h=h: e.activation(
                            out=s_rcp[0:1, :], in_=p_all[64:65, 4 + h, 0:512],
                            func=AF.Ln))
                        pr.wait("tensor", "SC", vln)
                        vb = pr.inc("tensor", "MM", lambda e, h=h: e.matmul(
                            p_all[0:64, 6 + h, 0:512], lhsT=s_one[0:1, 0:64],
                            rhs=s_rcp[0:1, :], start=True, stop=True))
                        vb_prev[0] = vb
                        pr.wait("scalar", "MM", vb)
                        if von_prev[0]:
                            pr.wait("scalar", "VE", von_prev[0])
                        vrb = pr.inc("scalar", "SC", lambda e, h=h: e.activation(
                            out=s_rb[:], in_=p_all[0:64, 6 + h, 0:512], func=AF.Exp,
                            scale=s_cst[0:64, C_NONE:C_NONE + 1]))
                        pr.wait("vector", "SC", vrb)
                        vVE_on[(hp, qb)] = pr.inc("vector", "VE",
                            lambda e, hp=hp, h=h, q0=q0: e.tensor_tensor(
                                out=s_qf[64 * h:64 * h + 64, hp, q0:q0 + 512],
                                in0=p_all[0:64, 4 + h, 0:512], in1=s_rb[:],
                                op=ALU.mult))
                        von_prev[0] = vVE_on[(hp, qb)]

            vMM_p2 = pr.ctr["MM"]
            shr = sh_o.rearrange("(r p) t -> r p t", p=128)
            pr.wait("sync", "VE", pr.ctr["VE"])
            vSX_shw = 0
            for c in range(4):
                def mk_shw(e, c=c):
                    rank = e.partition_id()
                    off = (rank % 2) * 4 + c
                    return e.dma_start(out=shr[bass.ts(off, 1)].squeeze(0),
                                       in_=s_qf[:, c, :])
                vSX_shw = pr.inc("sync", "SXO", mk_shw, 16)
            pr.wait("gpsimd", "SXO", vSX_shw)
            pr.inc("gpsimd", "SWF", lambda e: e.dma_start(out=flag_i, in_=s_one[0:1, 0:2]), 16)
            pr.wait("gpsimd", "SWF", None)
            vGP_fl = pr.inc("gpsimd", "GP", lambda e: e.collective_compute(
                "AllGather", ALU.bypass,
                replica_groups=[[0, 1], [2, 3], [4, 5], [6, 7]],
                ins=[flag_i.opt()], outs=[flag_o.opt()]))

        with ExitStack() as es3:
            sb3 = lambda n, s, d=F32, **kw: es3.enter_context(nc.sbuf_tensor(n, s, d, **kw))
            s_out = sb3("s_out", [128, 2, D])
            s_x1 = sb3("s_x1", [128, NOT, D])
            s_hsm = sb3("s_hsm", [128, 2, D], BF16)
            s_hsf = sb3("s_hsf", [128, 8, OWN], BF16, is_transpose_dest=True)
            s_lgt = sb3("s_lgt", [128, NOT, E])
            s_gl = sb3("s_gl", [128, GT, E])
            s_rmx = sb3("s_rmx", [128, GT, 1])
            s_msk = sb3("s_msk", [128, GT, E])
            s_zz = sb3("s_zz", [128, GT, 1])
            s_gate = sb3("s_gate", [128, GT])
            s_tot = sb3("s_tot", [GT, E])
            s_off = sb3("s_off", [GT, E])
            s_offr = sb3("s_offr", [1, GT * E])
            s_gm = sb3("s_gm", [128, NOT])

            pr.wait("sync", "GP", vGP_fl)
            es4 = ExitStack()
            s_oa = es4.enter_context(nc.sbuf_tensor("s_oa", [128, 8, OWN], BF16))

            def mk_shr(e):
                rank = e.partition_id()
                src = sh_o.rearrange("(c p) t -> p c t", p=128)
                return e.dma_start(out=s_oa[:],
                                   in_=src[:, :, bass.ts(rank % 2, OWN)])
            vSX_oa = pr.inc("sync", "SXA", mk_shr, 16)

            xo = x_own.rearrange("(n p) d -> n p d", p=128)
            vVE_x1 = [0] * NOT
            vVE_lge = [0] * NOT
            vSC_hs = [0] * NOT
            vSX_hsf = [0] * NOT
            vMM_lgt = [0] * NOT
            for tt in range(NOT):
                par = tt % 2
                if tt >= 2:
                    pr.wait("sync", "SC", vSC_hs[tt - 2])
                vSX_xo = pr.inc("sync", "SXX%d" % par, lambda e, tt=tt, par=par: e.dma_start(
                    out=s_x[:, par], in_=xo[tt]), 16)
                pr.wait("tensor", "SXA", vSX_oa)
                if tt == 0:
                    pr.wait("tensor", "SWC", v_wo)
                if tt >= 2:
                    pr.wait("tensor", "VE", vVE_x1[tt - 2])
                vMM_op = 0
                for ho in range(2):
                    b = 2 * par + ho
                    for c in range(8):
                        mk = lambda e, c=c, tt=tt, ho=ho, b=b: e.matmul(
                            bank(b), lhsT=s_oa[:, c, 128 * tt:128 * (tt + 1)],
                            rhs=s_wo[:, c, 512 * ho:512 * (ho + 1)],
                            start=(c == 0), stop=(c == 7))
                        if ho == 1 and c == 7:
                            vMM_op = pr.inc("tensor", "MM", mk)
                        else:
                            pr.op("tensor", mk)
                pr.wait("vector", "MM", vMM_op)
                pr.wait("vector", "SXX%d" % par, vSX_xo)
                for ho in range(2):
                    b = 2 * par + ho
                    mk = lambda e, tt=tt, ho=ho, b=b, par=par: e.tensor_tensor(
                        out=s_x1[:, tt, 512 * ho:512 * (ho + 1)],
                        in0=bank(b), in1=s_x[:, par, 512 * ho:512 * (ho + 1)],
                        op=ALU.add)
                    if ho == 1:
                        vVE_x1[tt] = pr.inc("vector", "VE", mk)
                    else:
                        pr.op("vector", mk)

                pr.wait("vector", "VE", vVE_x1[tt])
                v1 = pr.inc("vector", "VE", lambda e, tt=tt, par=par: e.reduce_sum(
                    out=s_st[:, par, 0:1], in_=s_x1[:, tt, :], axis=AX.X, negate=True))
                pr.wait("vector", "VE", v1)
                v2 = pr.inc("vector", "VE", lambda e, par=par: e.tensor_scalar_mul(
                    out=s_st[:, par, 1:2], in0=s_st[:, par, 0:1], scalar1=1.0 / D))
                pr.wait("vector", "VE", v2)
                v_xm = pr.inc("vector", "VE", lambda e, tt=tt, par=par: e.tensor_scalar(
                    out=s_x[:, par], in0=s_x1[:, tt, :], scalar1=s_st[:, par, 1:2],
                    scalar2=None, op0=ALU.add))
                pr.wait("scalar", "VE", v_xm)
                s1 = pr.inc("scalar", "SC", lambda e, par=par: e.activation(
                    out=s_out[:, 0, :], in_=s_x[:, par], func=AF.Square,
                    accum_out=s_st[:, par, 2:3]))
                pr.wait("scalar", "SC", s1)
                s2 = pr.inc("scalar", "SC", lambda e, par=par: e.activation(
                    out=s_st[:, par, 3:4], in_=s_st[:, par, 2:3], func=AF.Ln,
                    scale=cc(C_R1024), bias=cc(C_EPS)))
                pr.wait("scalar", "SC", s2)
                s3 = pr.inc("scalar", "SC", lambda e, par=par: e.activation(
                    out=s_st[:, par, 4:5], in_=s_st[:, par, 3:4], func=AF.Exp,
                    scale=cc(C_NHALF)))
                pr.wait("scalar", "SC", s3)
                if tt >= 2:
                    pr.wait("scalar", "STRL%d" % par, vSX_hsf[tt - 2])
                vSC_hs[tt] = pr.inc("scalar", "SC", lambda e, par=par: e.activation(
                    out=s_hsm[:, par], in_=s_x[:, par], func=AF.Copy,
                    scale=s_st[:, par, 4:5]))
                pr.wait("scalar", "SC", vSC_hs[tt])
                vSX_hsf[tt] = pr.inc("scalar", "STRL%d" % par, lambda e, tt=tt, par=par: e.dma_start(
                    out=s_hsf[:, :, 128 * tt:128 * (tt + 1)], in_=s_hsm[:, par],
                    transpose=True), 16)
                pr.wait("tensor", "STRL%d" % par, vSX_hsf[tt])
                if tt >= 2:
                    pr.wait("tensor", "VE", vVE_lge[tt - 2])
                for c in range(8):
                    mk = lambda e, c=c, tt=tt, par=par: e.matmul(
                        p_all[:, 4 + par, 0:E], lhsT=s_hsf[:, c, 128 * tt:128 * (tt + 1)],
                        rhs=s_wg[:, c, :], start=(c == 0), stop=(c == 7))
                    if c == 7:
                        vMM_lgt[tt] = pr.inc("tensor", "MM", mk)
                    else:
                        pr.op("tensor", mk)
                pr.wait("vector", "MM", vMM_lgt[tt])
                vVE_lge[tt] = pr.inc("vector", "VE", lambda e, tt=tt, par=par: e.tensor_copy(
                    out=s_lgt[:, tt, :], in_=p_all[:, 4 + par, 0:E]))

            es4.close()
            s_we = sb3("s_we", [128, 8, D], BF16)
            pr.wait("gpsimd", "MM", vMM_op)
            v_wall = g_dma(s_we[:], we_i.rearrange("(c p) j -> p c j", p=128), "SWD")

            pr.wait("sync", "VE", pr.ctr["VE"])
            vSX_lg = pr.inc("sync", "SXL", lambda e: e.dma_start(
                out=lg_i.rearrange("(n p) e -> p n e", p=128), in_=s_lgt[:]), 16)
            pr.wait("gpsimd", "SXL", vSX_lg)
            vGP_lg = pr.inc("gpsimd", "GP", lambda e: e.collective_compute(
                "AllGather", ALU.bypass, replica_groups=[[0, 1, 2, 3, 4, 5, 6, 7]],
                ins=[lg_i.opt()], outs=[lg_o.opt()]))
            pr.wait("sync", "GP", vGP_lg)
            vSX_gl = pr.inc("sync", "SXG", lambda e: e.dma_start(
                out=s_gl[:], in_=lg_o.rearrange("(g p) e -> p g e", p=128)), 16)

            pr.wait("vector", "SXG", vSX_gl)
            v1 = pr.inc("vector", "VE", lambda e: e.reduce_max(
                out=s_rmx[:], in_=s_gl[:], axis=AX.X))
            pr.wait("vector", "VE", v1)
            rbc = s_rmx[:].to_broadcast((128, GT, E))
            vmsk = pr.inc("vector", "VE", lambda e: e.tensor_tensor(
                out=s_msk[:], in0=s_gl[:], in1=rbc, op=ALU.is_equal))
            pr.wait("vector", "VE", vmsk)
            v2 = pr.inc("vector", "VE", lambda e: e.tensor_tensor(
                out=s_gl[:], in0=s_gl[:], in1=rbc, op=ALU.subtract))
            pr.wait("scalar", "VE", v2)
            s1 = pr.inc("scalar", "SC", lambda e: e.activation(
                out=s_gl[:], in_=s_gl[:], func=AF.Exp))
            pr.wait("vector", "SC", s1)
            v3 = pr.inc("vector", "VE", lambda e: e.reduce_sum(
                out=s_zz[:], in_=s_gl[:], axis=AX.X))
            pr.wait("vector", "VE", v3)
            v4 = pr.inc("vector", "VE", lambda e: e.reciprocal(
                out=s_gate[:].unsqueeze(2), in_=s_zz[:]))
            pv_rank = p_all[:, 0:2, :].rearrange("p b f -> p (b f)")[:, 0:GT * E].rearrange(
                "p (g e) -> p g e", e=E)
            msk_flat = s_msk[:].rearrange("p g e -> p (g e)")
            pr.wait("tensor", "MM", vMM_p2)
            pr.wait("tensor", "VE", pr.ctr["VE"])
            n_tchunk = (GT * E + 511) // 512
            vMM_tot = 0
            for j in range(n_tchunk):
                lo, hi = 512 * j, min(512 * (j + 1), GT * E)
                vMM_tot = pr.inc("tensor", "MM", lambda e, j=j, lo=lo, hi=hi: e.matmul(
                    p_all[0:1, 2 + j, 0:hi - lo], lhsT=s_tri[:, 0, 127:128],
                    rhs=msk_flat[:, lo:hi], start=True, stop=True))
            pr.wait("scalar", "MM", vMM_tot)
            s_tote = pr.inc("scalar", "SC", lambda e: e.activation(
                out=s_offr[0:1, :],
                in_=p_all[0:1, 2:2 + n_tchunk, :].rearrange(
                    "p b f -> p (b f)")[:, 0:GT * E], func=AF.Copy))
            pr.wait("sync", "SC", s_tote)
            v6 = pr.inc("sync", "SXR", lambda e: e.dma_start(
                out=scr.rearrange("g e -> (g e)").unsqueeze(0), in_=s_offr[0:1, :]), 16)
            pr.wait("sync", "SXR", v6)
            v7 = pr.inc("sync", "SXR", lambda e: e.dma_start(out=s_tot[:], in_=scr), 16)
            pr.wait("tensor", "SXR", v7)
            pr.wait("tensor", "SC", s_tote)
            vMM_of = pr.inc("tensor", "MM", lambda e: e.matmul(
                p_all[0:GT, 2, 0:E], lhsT=s_tri[0:GT, 1, 0:GT], rhs=s_tot[:],
                start=True, stop=True))
            pr.wait("scalar", "MM", vMM_of)
            s2 = pr.inc("scalar", "SC", lambda e: e.activation(
                out=s_off[:], in_=p_all[0:GT, 2, 0:E], func=AF.Copy))
            pr.wait("sync", "SC", s2)
            v8 = pr.inc("sync", "SXR", lambda e: e.dma_start(out=scr2, in_=s_off[:]), 16)
            pr.wait("sync", "SXR", v8)
            v9 = pr.inc("sync", "SXR", lambda e: e.dma_start(
                out=s_offr[:], in_=scr2.rearrange("g e -> (g e)").unsqueeze(0)), 16)
            pr.wait("tensor", "SXR", v9)
            vMM_rk2 = 0
            for g in range(GT):
                pr.op("tensor", lambda e, g=g: e.matmul(
                    pv_rank[:, g, :], lhsT=s_tri[:, 0], rhs=s_msk[:, g, :],
                    start=True, stop=False))
                vMM_rk2 = pr.inc("tensor", "MM", lambda e, g=g: e.matmul(
                    pv_rank[:, g, :], lhsT=s_one[:], rhs=s_offr[0:1, E * g:E * (g + 1)],
                    start=False, stop=True))
            pr.wait("vector", "MM", vMM_rk2)
            v9b = pr.inc("vector", "VE", lambda e: e.tensor_scalar(
                out=s_gl[:], in0=pv_rank, scalar1=cc(C_CAP), scalar2=None,
                op0=ALU.is_le))
            pr.wait("vector", "VE", v9b)
            v10 = pr.inc("vector", "VE", lambda e: e.tensor_tensor(
                out=s_gl[:], in0=s_gl[:], in1=s_msk[:], op=ALU.mult))
            pr.wait("vector", "VE", v10)
            v11 = pr.inc("vector", "VE", lambda e: e.reduce_sum(
                out=s_zz[:], in_=s_gl[:], axis=AX.X))
            pr.wait("vector", "VE", v11)
            v12 = pr.inc("vector", "VE", lambda e: e.tensor_tensor(
                out=s_gate[:], in0=s_gate[:], in1=s_zz[:].squeeze(2), op=ALU.mult))
            pr.wait("sync", "VE", v12)
            v13 = pr.inc("sync", "SXR", lambda e: e.dma_start(out=gate_d, in_=s_gate[:]), 16)
            pr.wait("sync", "SXR", v13)

            def mk_gm(e):
                rank = e.partition_id()
                return e.dma_start(out=s_gm[:], in_=gate_d[:, bass.ts(rank, NOT)])
            vSX_gm = pr.inc("sync", "SXR", mk_gm, 16)

            oo = out_o.rearrange("(n p) d -> n p d", p=128)
            for tt in range(NOT):
                par = tt % 2
                if tt == 0:
                    pr.wait("tensor", "MM", vMM_rk2)
                    pr.wait("tensor", "VE", pr.ctr["VE"])
                    pr.wait("tensor", "SWD", v_wall)
                if tt >= 2:
                    pr.wait("tensor", "VE", vVE_x1[tt - 2])
                vMM_y = 0
                for ho in range(2):
                    b = 2 * par + ho
                    for c in range(8):
                        pr.op("tensor", lambda e, c=c, tt=tt, ho=ho, b=b: e.matmul(
                            bank(b), lhsT=s_hsf[:, c, 128 * tt:128 * (tt + 1)],
                            rhs=s_we[:, c, 512 * ho:512 * (ho + 1)],
                            start=(c == 0), stop=False))
                    mkb = lambda e, ho=ho, b=b: e.matmul(
                        bank(b), lhsT=s_one[:], rhs=s_be[0:1, 512 * ho:512 * (ho + 1)],
                        start=False, stop=True)
                    if ho == 1:
                        vMM_y = pr.inc("tensor", "MM", mkb)
                    else:
                        pr.op("tensor", mkb)
                pr.wait("vector", "MM", vMM_y)
                pr.wait("vector", "SXR", vSX_gm)
                if tt >= 2:
                    pr.wait("vector", "ST%d" % par, 16 * (tt // 2))
                for ho in range(2):
                    b = 2 * par + ho
                    mk = lambda e, tt=tt, ho=ho, b=b, par=par: e.scalar_tensor_tensor(
                        out=s_out[:, par, 512 * ho:512 * (ho + 1)], in0=bank(b),
                        scalar=s_gm[:, tt:tt + 1],
                        in1=s_x1[:, tt, 512 * ho:512 * (ho + 1)],
                        op0=ALU.mult, op1=ALU.add)
                    if ho == 1:
                        vVE_x1[tt] = pr.inc("vector", "VE", mk)
                    else:
                        pr.op("vector", mk)
                pr.wait("sync", "VE", vVE_x1[tt])
                pr.inc("sync", "ST%d" % par, lambda e, tt=tt, par=par: e.dma_start(
                    out=oo[tt], in_=s_out[:, par]), 16)

            pr.wait("sync", "ST0", None)
            pr.wait("sync", "ST1", None)

        with nc.Block() as blk:
            pr.emit(blk)

    return nc


def host_tables(T):
    pos = np.arange(T, dtype=np.float64)
    i = np.arange(0, DH, 2, dtype=np.float64) / DH
    theta = pos[:, None] / (ROPE_BASE ** i[None, :])
    twopi = 2 * np.pi

    def red(th):
        r = np.mod(th + np.pi, twopi) - np.pi
        hi = r.astype(np.float32)
        lo = (r - hi.astype(np.float64)).astype(np.float32)
        return hi, lo

    a, b = red(theta)
    c, d = red(theta + np.pi / 2)
    return np.stack([a, b, c, d]).astype(np.float32)


def rope_perm():
    p = np.arange(512).reshape(8, 64)
    q = np.concatenate([p[:, 0::2], p[:, 1::2]], axis=1)
    return q.reshape(-1)


def make_inputs(full, T):
    (query, wq, wk, wv, wo, Aq, Bq, Av, Bv, w_gate, w_expert, b_expert) = full
    perm = rope_perm()
    th = host_tables(T)
    tri_i = np.triu(np.ones((128, 128), np.float32), 0)
    trs_i = np.triu(np.ones((128, 128), np.float32), 1)
    nid_i = (np.eye(128, dtype=np.float32) * -1e9)
    one_i = np.ones((1, 128), np.float32)
    cst = np.zeros((128, 16), np.float32)
    cst[:, C_EPS] = 1e-5
    cst[:, C_R1024] = 1.0 / D
    cst[:, C_NHALF] = -0.5
    cst[:, C_R8] = 0.125
    cst[:, C_SHIFT] = -12.0
    cst[:, C_CAP] = 100.5
    cst[:, C_NONE] = -1.0
    maps = []
    for c in range(8):
        b, hh = c // 2, c % 2
        sl = slice(512 * hh, 512 * (hh + 1))
        wq_s = np.ascontiguousarray(wq[:, sl][:, perm])
        wk_s = np.ascontiguousarray(wk[:, sl][:, perm])
        wv_s = np.ascontiguousarray(wv[:, sl])
        bq_s = np.ascontiguousarray(Bq[:, sl][:, perm])
        bv_s = np.ascontiguousarray(Bv[:, sl])
        xb = np.ascontiguousarray(query[b, :T])
        maps.append(dict(
            x_in=xb, x_own=np.ascontiguousarray(xb[hh * (T // 2):(hh + 1) * (T // 2)]),
            wq=wq_s, wk=wk_s, wv=wv_s, wo_i=np.ascontiguousarray(wo),
            aq=np.ascontiguousarray(Aq), av=np.ascontiguousarray(Av),
            bq_i=bq_s, bv_i=bv_s, wg_i=np.ascontiguousarray(w_gate),
            be_i=np.ascontiguousarray(b_expert.reshape(1, D)),
            we_i=np.ascontiguousarray(w_expert),
            th_i=np.ascontiguousarray(
                th[:, :T].reshape(4, T // 128, 128, 32).transpose(2, 0, 1, 3).reshape(128, -1)),
            tri_i=tri_i, trs_i=trs_i, nid_i=nid_i,
            one_i=one_i, cst_i=cst,
        ))
    return maps


def kernel(L0_attn_query, L0_attn_wq, L0_attn_wk, L0_attn_wv, L0_attn_wo,
           L0_attn_Aq, L0_attn_Bq, L0_attn_Av, L0_attn_Bv,
           L0_ffn_w_gate, L0_ffn_w_expert, L0_ffn_b_expert):
    T = L0_attn_query.shape[1]
    full = (np.asarray(L0_attn_query, np.float32), np.asarray(L0_attn_wq, np.float32),
            np.asarray(L0_attn_wk, np.float32), np.asarray(L0_attn_wv, np.float32),
            np.asarray(L0_attn_wo, np.float32), np.asarray(L0_attn_Aq, np.float32),
            np.asarray(L0_attn_Bq, np.float32), np.asarray(L0_attn_Av, np.float32),
            np.asarray(L0_attn_Bv, np.float32), np.asarray(L0_ffn_w_gate, np.float32),
            np.asarray(L0_ffn_w_expert, np.float32),
            np.asarray(L0_ffn_b_expert, np.float32))
    maps = make_inputs(full, T)
    nc = build_nc(T)
    res = run_bass_kernel_spmd(nc, maps, core_ids=list(range(8)))
    out = np.empty((B, T, D), np.float32)
    for c in range(8):
        b, hh = c // 2, c % 2
        out[b, hh * (T // 2):(hh + 1) * (T // 2)] = res.results[c]["out_o"]
    return out


# version 10
# speedup vs baseline: 1.9345x; 1.3729x over previous
import numpy as np
import concourse.bass as bass
import concourse.mybir as mybir
from concourse.bass_utils import run_bass_kernel_spmd

F32 = mybir.dt.float32
BF16 = mybir.dt.bfloat16
AF = mybir.ActivationFunctionType
ALU = mybir.AluOpType
AX = mybir.AxisListType

B, D = 4, 1024
NH, DH = 16, 64
E = 8
ROPE_BASE = 10000.0

ENGS = ("sync", "gpsimd", "tensor", "vector", "scalar")
C_EPS, C_R1024, C_NHALF, C_R8, C_SHIFT, C_CAP, C_NONE = 0, 1, 2, 3, 4, 5, 6


class Prog:

    def __init__(self, nc, es):
        self.nc = nc
        self.q = {e: [] for e in ENGS}
        self.ctr = {}
        self.h = {}
        self.es = es

    def sem(self, name):
        self.h[name] = self.es.enter_context(self.nc.semaphore(name))
        self.ctr[name] = 0

    def add(self, eng, fn):
        self.q[eng].append(fn)

    def inc(self, eng, name, mk, k=1):
        self.ctr[name] += k
        h, kk = self.h[name], k
        self.add(eng, lambda e: mk(e).then_inc(h, kk))
        return self.ctr[name]

    def op(self, eng, mk):
        self.add(eng, lambda e: mk(e))

    def wait(self, eng, name, v=None):
        h = self.h[name]
        v = self.ctr[name] if v is None else v
        assert v >= 0
        self.q[eng].append(lambda e, h=h, v=v: e.wait_ge(h, v))
        return v

    def emit(self, block):
        for eng in ENGS:
            ops = self.q[eng]
            getattr(block, eng)(lambda e, ops=ops: [f(e) for f in ops])


def build_nc(T):
    TOK = T
    OWN = T // 2
    NT = TOK // 128
    NOT = OWN // 128
    GT = (4 * T) // 128
    QB = 512
    NQB = TOK // QB
    NF = TOK // 128
    assert GT <= 128

    nc = bass.Bass("TRN2", target_bir_lowering=False)
    dt = lambda n, s, d=F32, **kw: nc.dram_tensor(n, s, d, **kw).ap()

    x_in = dt("x_in", [TOK, D], kind="ExternalInput")
    x_own = dt("x_own", [OWN, D], kind="ExternalInput")
    wq = dt("wq", [D, 512], kind="ExternalInput")
    wk = dt("wk", [D, 512], kind="ExternalInput")
    wv = dt("wv", [D, 512], kind="ExternalInput")
    wo_i = dt("wo_i", [D, D], kind="ExternalInput")
    aq = dt("aq", [D, E], kind="ExternalInput")
    av = dt("av", [D, E], kind="ExternalInput")
    bq_i = dt("bq_i", [E, 512], kind="ExternalInput")
    bv_i = dt("bv_i", [E, 512], kind="ExternalInput")
    wg_i = dt("wg_i", [D, E], kind="ExternalInput")
    be_i = dt("be_i", [1, D], kind="ExternalInput")
    we_i = dt("we_i", [D, D], kind="ExternalInput")
    th_i = dt("th_i", [128, 4 * (TOK // 128) * 32], kind="ExternalInput")
    tri_i = dt("tri_i", [128, 128], kind="ExternalInput")
    trs_i = dt("trs_i", [128, 128], kind="ExternalInput")
    nid_i = dt("nid_i", [128, 128], kind="ExternalInput")
    one_i = dt("one_i", [1, 128], kind="ExternalInput")
    cst_i = dt("cst_i", [128, 16], kind="ExternalInput")
    out_o = dt("out_o", [OWN, D], kind="ExternalOutput")

    sh_o = dt("sh_o", [D, TOK], BF16, kind="Internal", addr_space="Shared")
    flag_i = dt("flag_i", [1, 2], kind="Internal")
    flag_o = dt("flag_o", [2, 2], kind="Internal")
    lg_i = dt("lg_i", [OWN, E], kind="Internal")
    lg_o = dt("lg_o", [4 * T, E], kind="Internal")
    scr = dt("scr", [GT, E], kind="Internal")
    scr2 = dt("scr2", [GT, E], kind="Internal")
    gate_d = dt("gate_d", [128, GT], kind="Internal")

    from contextlib import ExitStack
    with ExitStack() as es0:
        sb0 = lambda n, s, d=F32, **kw: es0.enter_context(nc.sbuf_tensor(n, s, d, **kw))

        s_cst = sb0("s_cst", [128, 16])
        s_one = sb0("s_one", [1, 128])
        s_tri = sb0("s_tri", [128, 2, 128])
        s_trs = sb0("s_trs", [128, 128], BF16)
        s_nid = sb0("s_nid", [128, 128], BF16)
        s_x = sb0("s_x", [128, 2, D])
        s_st = sb0("s_st", [128, 2, 8])
        s_wo = sb0("s_wo", [128, 8, D], BF16)
        s_wg = sb0("s_wg", [128, 8, E], BF16)
        s_be = sb0("s_be", [1, D])
        s_sc = sb0("s_sc", [128, 2, NF, 32])

        p_all = es0.enter_context(nc.psum_tensor("p_all", [128, 8, 512], F32))
        bank = lambda i, lo=0, hi=512: p_all[:, i, lo:hi]

        pr = Prog(nc, es0)
        for s in ("SWA", "SWB", "SWC", "SWF", "SXX0", "SXX1", "STRL0", "STRL1",
                  "SXQ0", "SXQ1", "SXO", "SXA", "SXL", "SXG", "SXR", "SWD",
                  "ST0", "ST1", "MM", "VE", "SC", "GP"):
            pr.sem(s)
        cc = lambda i: s_cst[:, i:i + 1]

        def g_dma(dst, src, sem="SWA"):
            return pr.inc("gpsimd", sem,
                          lambda e, d=dst, s=src: e.dma_start(out=d, in_=s), 16)

        g_dma(s_cst[:], cst_i)
        g_dma(s_tri[:, 0], tri_i)
        g_dma(s_tri[:, 1], trs_i)
        g_dma(s_trs[:], trs_i)
        g_dma(s_nid[:], nid_i)
        g_dma(s_one[:], one_i)
        g_dma(s_wg[:], wg_i.rearrange("(c p) j -> p c j", p=128))
        g_dma(s_be[:], be_i)

        with ExitStack() as es1:
            sb1 = lambda n, s, d=F32, **kw: es1.enter_context(nc.sbuf_tensor(n, s, d, **kw))
            s_th = sb1("s_th", [128, 4, NF, 32])
            s_tm = sb1("s_tm", [128, 2, NF, 32])
            v_th = g_dma(s_th[:].rearrange("p k n f -> p (k n f)"), th_i)

            pr.wait("scalar", "SWA", v_th)
            pr.inc("scalar", "SC", lambda e: e.activation(
                out=s_sc[:, 0], in_=s_th[:, 0], func=AF.Sin))
            v_raw = pr.inc("scalar", "SC", lambda e: e.activation(
                out=s_sc[:, 1], in_=s_th[:, 2], func=AF.Sin))
            pr.wait("vector", "SC", v_raw)
            pr.op("vector", lambda e: e.tensor_tensor(
                out=s_tm[:, 0], in0=s_th[:, 1], in1=s_sc[:, 1], op=ALU.mult))
            v_m = pr.inc("vector", "VE", lambda e: e.tensor_tensor(
                out=s_tm[:, 1], in0=s_th[:, 3], in1=s_sc[:, 0], op=ALU.mult))
            pr.wait("vector", "VE", v_m)
            pr.op("vector", lambda e: e.tensor_tensor(
                out=s_sc[:, 0], in0=s_sc[:, 0], in1=s_tm[:, 0], op=ALU.add))
            v_tab = pr.inc("vector", "VE", lambda e: e.tensor_tensor(
                out=s_sc[:, 1], in0=s_sc[:, 1], in1=s_tm[:, 1], op=ALU.subtract))

        with ExitStack() as es2:
            sb2 = lambda n, s, d=F32, **kw: es2.enter_context(nc.sbuf_tensor(n, s, d, **kw))
            s_sq = sb2("s_sq", [128, D])
            s_ln = sb2("s_ln", [128, 2, D], BF16)
            s_lnf = sb2("s_lnf", [128, 2, 8, 128], BF16, is_transpose_dest=True)
            s_wq = sb2("s_wq", [128, 8, 512], BF16)
            s_wk = sb2("s_wk", [128, 8, 512], BF16)
            s_wv = sb2("s_wv", [128, 8, 512], BF16)
            s_aqv = sb2("s_aqv", [128, 8, 40], BF16)
            s_bq = sb2("s_bq", [E, 512])
            s_bv = sb2("s_bv", [E, 512])
            s_b4 = sb2("s_b4", [64, 512], BF16)
            s_t1 = sb2("s_t1", [64, 2, 128], BF16)
            s_rm = sb2("s_rm", [128, 4, 8, 32])
            s_qtm = sb2("s_qtm", [128, 2, 512], BF16)
            s_ktm = sb2("s_ktm", [128, 2, 512], BF16)
            s_qf = sb2("s_qf", [128, 4, TOK], BF16, is_transpose_dest=True)
            s_kf = sb2("s_kf", [128, 4, TOK], BF16, is_transpose_dest=True)
            s_v = sb2("s_v", [128, NT, 8, 65], BF16)
            s_pt = sb2("s_pt", [128, 2, 2, 512], BF16)
            s_rcp = sb2("s_rcp", [1, 512])
            s_rb = sb2("s_rb", [64, 512])

            pr.wait("gpsimd", "VE", v_tab)
            g_dma(s_wq[:], wq.rearrange("(c p) j -> p c j", p=128), "SWB")
            g_dma(s_wk[:], wk.rearrange("(c p) j -> p c j", p=128), "SWB")
            g_dma(s_wv[:], wv.rearrange("(c p) j -> p c j", p=128), "SWB")
            g_dma(s_aqv[:, :, 0:8], aq.rearrange("(c p) r -> p c r", p=128), "SWB")
            g_dma(s_aqv[:, :, 32:40], av.rearrange("(c p) r -> p c r", p=128), "SWB")
            g_dma(s_bq[:], bq_i, "SWB")
            v_w1 = g_dma(s_bv[:], bv_i, "SWB")
            v_wo = g_dma(s_wo[:], wo_i.rearrange("(c p) j -> p c j", p=128), "SWC")

            pr.wait("vector", "SWB", v_w1)
            pr.op("vector", lambda e: e.memset(s_aqv[:, :, 8:32], 0.0))
            pr.op("vector", lambda e: e.tensor_scalar_mul(
                out=s_b4[0:8, :], in0=s_bq[:], scalar1=4.0))
            pr.op("vector", lambda e: e.tensor_scalar_mul(
                out=s_b4[32:40, :], in0=s_bv[:], scalar1=4.0))
            v_vone = pr.inc("vector", "VE", lambda e: e.memset(s_v[:, :, :, 64:65], 1.0))

            xr = x_in.rearrange("(n p) d -> n p d", p=128)
            vSX_x = [0] * NT
            vSC_ln = [0] * NT
            vSC_vev = [0] * NT
            vVE_rope = [0] * NT
            vSX_lnf = [0] * NT
            vSX_qk = [0] * NT
            vMM_qkv = [0] * NT
            vMM_t1 = [0] * NT
            vVE_t1e = [0] * NT

            def p1_load(i):
                par = i % 2
                if i >= 2:
                    pr.wait("sync", "SC", vSC_ln[i - 2])
                vSX_x[i] = pr.inc("sync", "SXX%d" % par,
                                  lambda e, i=i, par=par: e.dma_start(
                                      out=s_x[:, par], in_=xr[i]), 16)

            def p1_front(i):
                par = i % 2
                pr.wait("vector", "SXX%d" % par, vSX_x[i])
                v1 = pr.inc("vector", "VE", lambda e, par=par: e.reduce_sum(
                    out=s_st[:, par, 0:1], in_=s_x[:, par], axis=AX.X, negate=True))
                pr.wait("vector", "VE", v1)
                v2 = pr.inc("vector", "VE", lambda e, par=par: e.tensor_scalar_mul(
                    out=s_st[:, par, 1:2], in0=s_st[:, par, 0:1], scalar1=1.0 / D))
                pr.wait("vector", "VE", v2)
                v_xm = pr.inc("vector", "VE", lambda e, par=par: e.tensor_scalar_add(
                    out=s_x[:, par], in0=s_x[:, par], scalar1=s_st[:, par, 1:2]))
                pr.wait("scalar", "VE", v_xm)
                s1 = pr.inc("scalar", "SC", lambda e, par=par: e.activation(
                    out=s_sq[:], in_=s_x[:, par], func=AF.Square,
                    accum_out=s_st[:, par, 2:3]))
                pr.wait("scalar", "SC", s1)
                s2 = pr.inc("scalar", "SC", lambda e, par=par: e.activation(
                    out=s_st[:, par, 3:4], in_=s_st[:, par, 2:3], func=AF.Ln,
                    scale=cc(C_R1024), bias=cc(C_EPS)))
                pr.wait("scalar", "SC", s2)
                s3 = pr.inc("scalar", "SC", lambda e, par=par: e.activation(
                    out=s_st[:, par, 4:5], in_=s_st[:, par, 3:4], func=AF.Exp,
                    scale=cc(C_NHALF)))
                pr.wait("scalar", "SC", s3)
                if i >= 2:
                    pr.wait("scalar", "STRL%d" % par, vSX_lnf[i - 2])
                vSC_ln[i] = pr.inc("scalar", "SC", lambda e, par=par: e.activation(
                    out=s_ln[:, par], in_=s_x[:, par], func=AF.Copy,
                    scale=s_st[:, par, 4:5]))

                pr.wait("scalar", "SC", vSC_ln[i])
                if i >= 2:
                    pr.wait("scalar", "MM", vMM_qkv[i - 2])
                vSX_lnf[i] = pr.inc("scalar", "STRL%d" % par, lambda e, par=par: e.dma_start(
                    out=s_lnf[:, par], in_=s_ln[:, par], transpose=True), 16)

            def p1_back(i):
                par = i % 2
                bq_b, bk_b, bv_b, bt_b = 4 * par + 0, 4 * par + 1, 4 * par + 2, 4 * par + 3
                pr.wait("tensor", "STRL%d" % par, vSX_lnf[i])
                if i == 0:
                    pr.wait("tensor", "SWB", v_w1)
                    pr.wait("tensor", "SWA", None)
                    pr.wait("tensor", "VE", v_vone)
                if i >= 2:
                    pr.wait("tensor", "VE", vVE_rope[i - 2])
                    pr.wait("tensor", "SC", vSC_vev[i - 2])
                    pr.wait("tensor", "VE", vVE_t1e[i - 2])
                for c in range(7):
                    pr.op("tensor", lambda e, c=c, par=par, bt=bt_b: e.matmul(
                        p_all[0:40, bt, 0:128], lhsT=s_aqv[:, c, :],
                        rhs=s_lnf[:, par, c, :], start=(c == 0), stop=False))
                vMM_t1[i] = pr.inc("tensor", "MM", lambda e, par=par, bt=bt_b: e.matmul(
                    p_all[0:40, bt, 0:128], lhsT=s_aqv[:, 7, :],
                    rhs=s_lnf[:, par, 7, :], start=False, stop=True))
                pr.wait("vector", "MM", vMM_t1[i])
                vVE_t1e[i] = pr.inc("vector", "VE", lambda e, par=par, bt=bt_b: e.tensor_copy(
                    out=s_t1[0:40, par, :], in_=p_all[0:40, bt, 0:128]))
                for c in range(8):
                    pr.op("tensor", lambda e, c=c, par=par, b=bq_b: e.matmul(
                        bank(b), lhsT=s_lnf[:, par, c, :], rhs=s_wq[:, c, :],
                        start=(c == 0), stop=False))
                    pr.op("tensor", lambda e, c=c, par=par, b=bk_b: e.matmul(
                        bank(b), lhsT=s_lnf[:, par, c, :], rhs=s_wk[:, c, :],
                        start=(c == 0), stop=(c == 7)))
                    pr.op("tensor", lambda e, c=c, par=par, b=bv_b: e.matmul(
                        bank(b), lhsT=s_lnf[:, par, c, :], rhs=s_wv[:, c, :],
                        start=(c == 0), stop=False))
                pr.wait("tensor", "VE", vVE_t1e[i])
                pr.op("tensor", lambda e, par=par, b=bq_b: e.matmul(
                    bank(b), lhsT=s_t1[0:8, par, :], rhs=s_b4[0:8, :],
                    start=False, stop=True))
                vMM_qkv[i] = pr.inc("tensor", "MM", lambda e, par=par, b=bv_b: e.matmul(
                    bank(b), lhsT=s_t1[32:40, par, :], rhs=s_b4[32:40, :],
                    start=False, stop=True))

                if i == 0:
                    pr.wait("vector", "VE", v_tab)
                pr.wait("vector", "MM", vMM_qkv[i])
                if i >= 2:
                    pr.wait("vector", "SXQ%d" % par, vSX_qk[i - 2])
                co = s_sc[:, 1, i:i + 1, :].to_broadcast((128, 8, 32))
                si = s_sc[:, 0, i:i + 1, :].to_broadcast((128, 8, 32))
                for (bnk, stm) in ((bq_b, s_qtm), (bk_b, s_ktm)):
                    pv = p_all[:, bnk, :].rearrange("p (h c) -> p h c", h=8)
                    qE, qO = pv[:, :, 0:32], pv[:, :, 32:64]
                    ot = stm[:, par, :].rearrange("p (h c) -> p h c", h=8)
                    pr.op("vector", lambda e, qE=qE, co=co: e.tensor_tensor(
                        out=s_rm[:, 0], in0=qE, in1=co, op=ALU.mult))
                    pr.op("vector", lambda e, qO=qO, si=si: e.tensor_tensor(
                        out=s_rm[:, 1], in0=qO, in1=si, op=ALU.mult))
                    pr.op("vector", lambda e, qE=qE, si=si: e.tensor_tensor(
                        out=s_rm[:, 2], in0=qE, in1=si, op=ALU.mult))
                    v3 = pr.inc("vector", "VE", lambda e, qO=qO, co=co: e.tensor_tensor(
                        out=s_rm[:, 3], in0=qO, in1=co, op=ALU.mult))
                    pr.wait("vector", "VE", v3)
                    pr.op("vector", lambda e, ot=ot: e.tensor_tensor(
                        out=ot[:, :, 0:32], in0=s_rm[:, 0], in1=s_rm[:, 1],
                        op=ALU.subtract))
                    v4 = pr.inc("vector", "VE", lambda e, ot=ot: e.tensor_tensor(
                        out=ot[:, :, 32:64], in0=s_rm[:, 2], in1=s_rm[:, 3],
                        op=ALU.add))
                    pr.wait("vector", "VE", v4)
                vVE_rope[i] = v4
                pr.wait("scalar", "MM", vMM_qkv[i])
                vSC_vev[i] = pr.inc("scalar", "SC", lambda e, par=par, i=i, b=bv_b: e.activation(
                    out=s_v[:, i, :, 0:64],
                    in_=bank(b).rearrange("p (h c) -> p h c", h=8), func=AF.Copy))

                pr.wait("sync", "VE", vVE_rope[i])
                pr.inc("sync", "SXQ%d" % par, lambda e, par=par, i=i: e.dma_start(
                    out=s_qf[:, :, 128 * i:128 * (i + 1)], in_=s_qtm[:, par, :],
                    transpose=True), 16)
                vSX_qk[i] = pr.inc("sync", "SXQ%d" % par, lambda e, par=par, i=i: e.dma_start(
                    out=s_kf[:, :, 128 * i:128 * (i + 1)], in_=s_ktm[:, par, :],
                    transpose=True), 16)

            p1_load(0)
            p1_load(1)
            for i in range(NT):
                p1_front(i)
                if i + 2 < NT:
                    p1_load(i + 2)
                if i >= 1:
                    p1_back(i - 1)
            p1_back(NT - 1)

            vSC_p1 = pr.ctr["SC"]

            vSC_exp = {}
            vMM_att = {}
            vVE_on = {}
            vb_prev = [0]
            von_prev = [0]
            pr.wait("tensor", "SXQ0", None)
            pr.wait("tensor", "SXQ1", None)
            pr.wait("tensor", "SC", vSC_p1)
            for hp in range(4):
                for qb in range(NQB):
                    q0 = qb * QB
                    nkt = q0 // 128 + 4
                    def emit_scores_exp(kt):
                        par = kt % 2
                        koff = kt * 128
                        qlo = max(0, koff - q0)
                        if (hp, qb, kt - 2) in vSC_exp:
                            pr.wait("tensor", "SC", vSC_exp[(hp, qb, kt - 2)])
                        elif qb > 0 and kt < 2:
                            k_prev = (q0 - QB) // 128 + 4
                            if (hp, qb - 1, k_prev - 2 + kt) in vSC_exp:
                                pr.wait("tensor", "SC",
                                        vSC_exp[(hp, qb - 1, k_prev - 2 + kt)])
                        elif hp > 0 and qb == 0 and kt < 2:
                            pr.wait("tensor", "SC",
                                    vSC_exp[(hp - 1, NQB - 1, (NQB - 1) * QB // 128 + 2 + kt)])
                        diag = koff >= q0
                        vS = None
                        for h in range(2):
                            mk1 = lambda e, hp=hp, h=h, par=par, koff=koff, \
                                  q0=q0, qlo=qlo, diag=diag: e.matmul(
                                p_all[:, 2 * par + h, qlo:512],
                                lhsT=s_kf[64 * h:64 * h + 64, hp, koff:koff + 128],
                                rhs=s_qf[64 * h:64 * h + 64, hp, q0 + qlo:q0 + 512],
                                start=True, stop=not diag)
                            mk2 = lambda e, h=h, par=par, qlo=qlo: e.matmul(
                                p_all[:, 2 * par + h, qlo:qlo + 128],
                                lhsT=s_trs[:], rhs=s_nid[:],
                                start=False, stop=True)
                            if diag:
                                pr.op("tensor", mk1)
                                if h == 1:
                                    vS = pr.inc("tensor", "MM", mk2)
                                else:
                                    pr.op("tensor", mk2)
                            else:
                                if h == 1:
                                    vS = pr.inc("tensor", "MM", mk1)
                                else:
                                    pr.op("tensor", mk1)
                        pr.wait("scalar", "MM", vS)
                        if (hp, qb, kt - 2) in vMM_att:
                            pr.wait("scalar", "MM", vMM_att[(hp, qb, kt - 2)])
                        vSC_exp[(hp, qb, kt)] = pr.inc("scalar", "SC",
                            lambda e, par=par, qlo=qlo: e.activation(
                                out=s_pt[:, par, :, qlo:512],
                                in_=p_all[:, 2 * par:2 * par + 2, qlo:512],
                                func=AF.Exp, scale=cc(C_R8), bias=cc(C_SHIFT)))

                    def emit_attnv(kt):
                        par = kt % 2
                        koff = kt * 128
                        qlo = max(0, koff - q0)
                        pr.wait("tensor", "SC", vSC_exp[(hp, qb, kt)])
                        if kt == 0 and (hp, qb - 1) in vVE_on:
                            pr.wait("tensor", "VE", vVE_on[(hp, qb - 1)])
                        if kt == 0 and qb == 0 and (hp - 1, NQB - 1) in vVE_on:
                            pr.wait("tensor", "VE", vVE_on[(hp - 1, NQB - 1)])
                        for h in range(2):
                            mk = lambda e, hp=hp, h=h, par=par, kt=kt, qlo=qlo, nkt=nkt: e.matmul(
                                p_all[0:65, 4 + h, qlo:512],
                                lhsT=s_v[:, kt, 2 * hp + h, :],
                                rhs=s_pt[:, par, h, qlo:512],
                                start=(kt == 0), stop=(kt == nkt - 1))
                            if h == 1:
                                vMM_att[(hp, qb, kt)] = pr.inc("tensor", "MM", mk)
                            else:
                                pr.op("tensor", mk)

                    for kt in range(nkt):
                        emit_scores_exp(kt)
                        if kt >= 1:
                            emit_attnv(kt - 1)
                    emit_attnv(nkt - 1)
                    vlast = vMM_att[(hp, qb, nkt - 1)]
                    for h in range(2):
                        pr.wait("scalar", "MM", vlast)
                        if vb_prev[0]:
                            pr.wait("scalar", "MM", vb_prev[0])
                        vln = pr.inc("scalar", "SC", lambda e, h=h: e.activation(
                            out=s_rcp[0:1, :], in_=p_all[64:65, 4 + h, 0:512],
                            func=AF.Ln))
                        pr.wait("tensor", "SC", vln)
                        vb = pr.inc("tensor", "MM", lambda e, h=h: e.matmul(
                            p_all[0:64, 6 + h, 0:512], lhsT=s_one[0:1, 0:64],
                            rhs=s_rcp[0:1, :], start=True, stop=True))
                        vb_prev[0] = vb
                        pr.wait("scalar", "MM", vb)
                        if von_prev[0]:
                            pr.wait("scalar", "VE", von_prev[0])
                        vrb = pr.inc("scalar", "SC", lambda e, h=h: e.activation(
                            out=s_rb[:], in_=p_all[0:64, 6 + h, 0:512], func=AF.Exp,
                            scale=s_cst[0:64, C_NONE:C_NONE + 1]))
                        pr.wait("vector", "SC", vrb)
                        vVE_on[(hp, qb)] = pr.inc("vector", "VE",
                            lambda e, hp=hp, h=h, q0=q0: e.tensor_tensor(
                                out=s_qf[64 * h:64 * h + 64, hp, q0:q0 + 512],
                                in0=p_all[0:64, 4 + h, 0:512], in1=s_rb[:],
                                op=ALU.mult))
                        von_prev[0] = vVE_on[(hp, qb)]

            vMM_p2 = pr.ctr["MM"]
            shr = sh_o.rearrange("(r p) t -> r p t", p=128)
            pr.wait("sync", "VE", pr.ctr["VE"])
            vSX_shw = 0
            for c in range(4):
                def mk_shw(e, c=c):
                    rank = e.partition_id()
                    off = (rank % 2) * 4 + c
                    return e.dma_start(out=shr[bass.ts(off, 1)].squeeze(0),
                                       in_=s_qf[:, c, :])
                vSX_shw = pr.inc("sync", "SXO", mk_shw, 16)
            pr.wait("gpsimd", "SXO", vSX_shw)
            pr.inc("gpsimd", "SWF", lambda e: e.dma_start(out=flag_i, in_=s_one[0:1, 0:2]), 16)
            pr.wait("gpsimd", "SWF", None)
            vGP_fl = pr.inc("gpsimd", "GP", lambda e: e.collective_compute(
                "AllGather", ALU.bypass,
                replica_groups=[[0, 1], [2, 3], [4, 5], [6, 7]],
                ins=[flag_i.opt()], outs=[flag_o.opt()]))

        with ExitStack() as es3:
            sb3 = lambda n, s, d=F32, **kw: es3.enter_context(nc.sbuf_tensor(n, s, d, **kw))
            s_out = sb3("s_out", [128, 2, D])
            s_x1 = sb3("s_x1", [128, NOT, D])
            s_hsm = sb3("s_hsm", [128, 2, D], BF16)
            s_hsf = sb3("s_hsf", [128, 8, OWN], BF16, is_transpose_dest=True)
            s_lgt = sb3("s_lgt", [128, NOT, E])
            s_gl = sb3("s_gl", [128, GT, E])
            s_rmx = sb3("s_rmx", [128, GT, 1])
            s_msk = sb3("s_msk", [128, GT, E])
            s_zz = sb3("s_zz", [128, GT, 1])
            s_gate = sb3("s_gate", [128, GT])
            s_tot = sb3("s_tot", [GT, E])
            s_off = sb3("s_off", [GT, E])
            s_offr = sb3("s_offr", [1, GT * E])
            s_gm = sb3("s_gm", [128, NOT])

            pr.wait("sync", "GP", vGP_fl)
            es4 = ExitStack()
            s_oa = es4.enter_context(nc.sbuf_tensor("s_oa", [128, 8, OWN], BF16))

            def mk_shr(e):
                rank = e.partition_id()
                src = sh_o.rearrange("(c p) t -> p c t", p=128)
                return e.dma_start(out=s_oa[:],
                                   in_=src[:, :, bass.ts(rank % 2, OWN)])
            vSX_oa = pr.inc("sync", "SXA", mk_shr, 16)

            xo = x_own.rearrange("(n p) d -> n p d", p=128)
            vVE_x1 = [0] * NOT
            vVE_lge = [0] * NOT
            vSX_xo = [0] * NOT
            vMM_op = [0] * NOT
            vSC_hs = [0] * NOT
            vSX_hsf = [0] * NOT
            vMM_lgt = [0] * NOT
            def p3_load(tt):
                par = tt % 2
                if tt >= 2:
                    pr.wait("sync", "SC", vSC_hs[tt - 2])
                vSX_xo[tt] = pr.inc("sync", "SXX%d" % par, lambda e, tt=tt, par=par: e.dma_start(
                    out=s_x[:, par], in_=xo[tt]), 16)

            def p3_front(tt):
                par = tt % 2
                pr.wait("tensor", "SXA", vSX_oa)
                if tt == 0:
                    pr.wait("tensor", "SWC", v_wo)
                if tt >= 2:
                    pr.wait("tensor", "VE", vVE_x1[tt - 2])
                for ho in range(2):
                    b = 2 * par + ho
                    for c in range(8):
                        mk = lambda e, c=c, tt=tt, ho=ho, b=b: e.matmul(
                            bank(b), lhsT=s_oa[:, c, 128 * tt:128 * (tt + 1)],
                            rhs=s_wo[:, c, 512 * ho:512 * (ho + 1)],
                            start=(c == 0), stop=(c == 7))
                        if ho == 1 and c == 7:
                            vMM_op[tt] = pr.inc("tensor", "MM", mk)
                        else:
                            pr.op("tensor", mk)
                pr.wait("vector", "MM", vMM_op[tt])
                pr.wait("vector", "SXX%d" % par, vSX_xo[tt])
                for ho in range(2):
                    b = 2 * par + ho
                    mk = lambda e, tt=tt, ho=ho, b=b, par=par: e.tensor_tensor(
                        out=s_x1[:, tt, 512 * ho:512 * (ho + 1)],
                        in0=bank(b), in1=s_x[:, par, 512 * ho:512 * (ho + 1)],
                        op=ALU.add)
                    if ho == 1:
                        vVE_x1[tt] = pr.inc("vector", "VE", mk)
                    else:
                        pr.op("vector", mk)

                pr.wait("vector", "VE", vVE_x1[tt])
                v1 = pr.inc("vector", "VE", lambda e, tt=tt, par=par: e.reduce_sum(
                    out=s_st[:, par, 0:1], in_=s_x1[:, tt, :], axis=AX.X, negate=True))
                pr.wait("vector", "VE", v1)
                v2 = pr.inc("vector", "VE", lambda e, par=par: e.tensor_scalar_mul(
                    out=s_st[:, par, 1:2], in0=s_st[:, par, 0:1], scalar1=1.0 / D))
                pr.wait("vector", "VE", v2)
                v_xm = pr.inc("vector", "VE", lambda e, tt=tt, par=par: e.tensor_scalar(
                    out=s_x[:, par], in0=s_x1[:, tt, :], scalar1=s_st[:, par, 1:2],
                    scalar2=None, op0=ALU.add))
                pr.wait("scalar", "VE", v_xm)
                s1 = pr.inc("scalar", "SC", lambda e, par=par: e.activation(
                    out=s_out[:, 0, :], in_=s_x[:, par], func=AF.Square,
                    accum_out=s_st[:, par, 2:3]))
                pr.wait("scalar", "SC", s1)
                s2 = pr.inc("scalar", "SC", lambda e, par=par: e.activation(
                    out=s_st[:, par, 3:4], in_=s_st[:, par, 2:3], func=AF.Ln,
                    scale=cc(C_R1024), bias=cc(C_EPS)))
                pr.wait("scalar", "SC", s2)
                s3 = pr.inc("scalar", "SC", lambda e, par=par: e.activation(
                    out=s_st[:, par, 4:5], in_=s_st[:, par, 3:4], func=AF.Exp,
                    scale=cc(C_NHALF)))
                pr.wait("scalar", "SC", s3)
                if tt >= 2:
                    pr.wait("scalar", "STRL%d" % par, vSX_hsf[tt - 2])
                vSC_hs[tt] = pr.inc("scalar", "SC", lambda e, par=par: e.activation(
                    out=s_hsm[:, par], in_=s_x[:, par], func=AF.Copy,
                    scale=s_st[:, par, 4:5]))
                pr.wait("scalar", "SC", vSC_hs[tt])
                vSX_hsf[tt] = pr.inc("scalar", "STRL%d" % par, lambda e, tt=tt, par=par: e.dma_start(
                    out=s_hsf[:, :, 128 * tt:128 * (tt + 1)], in_=s_hsm[:, par],
                    transpose=True), 16)

            def p3_back(tt):
                par = tt % 2
                pr.wait("tensor", "STRL%d" % par, vSX_hsf[tt])
                if tt >= 2:
                    pr.wait("tensor", "VE", vVE_lge[tt - 2])
                for c in range(8):
                    mk = lambda e, c=c, tt=tt, par=par: e.matmul(
                        p_all[:, 4 + par, 0:E], lhsT=s_hsf[:, c, 128 * tt:128 * (tt + 1)],
                        rhs=s_wg[:, c, :], start=(c == 0), stop=(c == 7))
                    if c == 7:
                        vMM_lgt[tt] = pr.inc("tensor", "MM", mk)
                    else:
                        pr.op("tensor", mk)
                pr.wait("vector", "MM", vMM_lgt[tt])
                vVE_lge[tt] = pr.inc("vector", "VE", lambda e, tt=tt, par=par: e.tensor_copy(
                    out=s_lgt[:, tt, :], in_=p_all[:, 4 + par, 0:E]))

            p3_load(0)
            p3_load(1)
            for tt in range(NOT):
                p3_front(tt)
                if tt + 2 < NOT:
                    p3_load(tt + 2)
                if tt >= 1:
                    p3_back(tt - 1)
            p3_back(NOT - 1)

            es4.close()
            s_we = sb3("s_we", [128, 8, D], BF16)
            pr.wait("gpsimd", "MM", vMM_op[NOT - 1])
            v_wall = g_dma(s_we[:], we_i.rearrange("(c p) j -> p c j", p=128), "SWD")

            pr.wait("sync", "VE", pr.ctr["VE"])
            vSX_lg = pr.inc("sync", "SXL", lambda e: e.dma_start(
                out=lg_i.rearrange("(n p) e -> p n e", p=128), in_=s_lgt[:]), 16)
            pr.wait("gpsimd", "SXL", vSX_lg)
            vGP_lg = pr.inc("gpsimd", "GP", lambda e: e.collective_compute(
                "AllGather", ALU.bypass, replica_groups=[[0, 1, 2, 3, 4, 5, 6, 7]],
                ins=[lg_i.opt()], outs=[lg_o.opt()]))
            pr.wait("sync", "GP", vGP_lg)
            vSX_gl = pr.inc("sync", "SXG", lambda e: e.dma_start(
                out=s_gl[:], in_=lg_o.rearrange("(g p) e -> p g e", p=128)), 16)

            pr.wait("vector", "SXG", vSX_gl)
            v1 = pr.inc("vector", "VE", lambda e: e.reduce_max(
                out=s_rmx[:], in_=s_gl[:], axis=AX.X))
            pr.wait("vector", "VE", v1)
            rbc = s_rmx[:].to_broadcast((128, GT, E))
            vmsk = pr.inc("vector", "VE", lambda e: e.tensor_tensor(
                out=s_msk[:], in0=s_gl[:], in1=rbc, op=ALU.is_equal))
            pr.wait("vector", "VE", vmsk)
            v2 = pr.inc("vector", "VE", lambda e: e.tensor_tensor(
                out=s_gl[:], in0=s_gl[:], in1=rbc, op=ALU.subtract))
            pr.wait("scalar", "VE", v2)
            s1 = pr.inc("scalar", "SC", lambda e: e.activation(
                out=s_gl[:], in_=s_gl[:], func=AF.Exp))
            pr.wait("vector", "SC", s1)
            v3 = pr.inc("vector", "VE", lambda e: e.reduce_sum(
                out=s_zz[:], in_=s_gl[:], axis=AX.X))
            pr.wait("vector", "VE", v3)
            v4 = pr.inc("vector", "VE", lambda e: e.reciprocal(
                out=s_gate[:].unsqueeze(2), in_=s_zz[:]))
            pv_rank = p_all[:, 0:2, :].rearrange("p b f -> p (b f)")[:, 0:GT * E].rearrange(
                "p (g e) -> p g e", e=E)
            msk_flat = s_msk[:].rearrange("p g e -> p (g e)")
            pr.wait("tensor", "MM", vMM_p2)
            pr.wait("tensor", "VE", pr.ctr["VE"])
            n_tchunk = (GT * E + 511) // 512
            vMM_tot = 0
            for j in range(n_tchunk):
                lo, hi = 512 * j, min(512 * (j + 1), GT * E)
                vMM_tot = pr.inc("tensor", "MM", lambda e, j=j, lo=lo, hi=hi: e.matmul(
                    p_all[0:1, 2 + j, 0:hi - lo], lhsT=s_tri[:, 0, 127:128],
                    rhs=msk_flat[:, lo:hi], start=True, stop=True))
            pr.wait("scalar", "MM", vMM_tot)
            s_tote = pr.inc("scalar", "SC", lambda e: e.activation(
                out=s_offr[0:1, :],
                in_=p_all[0:1, 2:2 + n_tchunk, :].rearrange(
                    "p b f -> p (b f)")[:, 0:GT * E], func=AF.Copy))
            pr.wait("sync", "SC", s_tote)
            v6 = pr.inc("sync", "SXR", lambda e: e.dma_start(
                out=scr.rearrange("g e -> (g e)").unsqueeze(0), in_=s_offr[0:1, :]), 16)
            pr.wait("sync", "SXR", v6)
            v7 = pr.inc("sync", "SXR", lambda e: e.dma_start(out=s_tot[:], in_=scr), 16)
            pr.wait("tensor", "SXR", v7)
            pr.wait("tensor", "SC", s_tote)
            vMM_of = pr.inc("tensor", "MM", lambda e: e.matmul(
                p_all[0:GT, 2, 0:E], lhsT=s_tri[0:GT, 1, 0:GT], rhs=s_tot[:],
                start=True, stop=True))
            pr.wait("scalar", "MM", vMM_of)
            s2 = pr.inc("scalar", "SC", lambda e: e.activation(
                out=s_off[:], in_=p_all[0:GT, 2, 0:E], func=AF.Copy))
            pr.wait("sync", "SC", s2)
            v8 = pr.inc("sync", "SXR", lambda e: e.dma_start(out=scr2, in_=s_off[:]), 16)
            pr.wait("sync", "SXR", v8)
            v9 = pr.inc("sync", "SXR", lambda e: e.dma_start(
                out=s_offr[:], in_=scr2.rearrange("g e -> (g e)").unsqueeze(0)), 16)
            pr.wait("tensor", "SXR", v9)
            vMM_rk2 = 0
            for g in range(GT):
                pr.op("tensor", lambda e, g=g: e.matmul(
                    pv_rank[:, g, :], lhsT=s_tri[:, 0], rhs=s_msk[:, g, :],
                    start=True, stop=False))
                vMM_rk2 = pr.inc("tensor", "MM", lambda e, g=g: e.matmul(
                    pv_rank[:, g, :], lhsT=s_one[:], rhs=s_offr[0:1, E * g:E * (g + 1)],
                    start=False, stop=True))
            pr.wait("vector", "MM", vMM_rk2)
            v9b = pr.inc("vector", "VE", lambda e: e.tensor_scalar(
                out=s_gl[:], in0=pv_rank, scalar1=cc(C_CAP), scalar2=None,
                op0=ALU.is_le))
            pr.wait("vector", "VE", v9b)
            v10 = pr.inc("vector", "VE", lambda e: e.tensor_tensor(
                out=s_gl[:], in0=s_gl[:], in1=s_msk[:], op=ALU.mult))
            pr.wait("vector", "VE", v10)
            v11 = pr.inc("vector", "VE", lambda e: e.reduce_sum(
                out=s_zz[:], in_=s_gl[:], axis=AX.X))
            pr.wait("vector", "VE", v11)
            v12 = pr.inc("vector", "VE", lambda e: e.tensor_tensor(
                out=s_gate[:], in0=s_gate[:], in1=s_zz[:].squeeze(2), op=ALU.mult))
            pr.wait("sync", "VE", v12)
            v13 = pr.inc("sync", "SXR", lambda e: e.dma_start(out=gate_d, in_=s_gate[:]), 16)
            pr.wait("sync", "SXR", v13)

            def mk_gm(e):
                rank = e.partition_id()
                return e.dma_start(out=s_gm[:], in_=gate_d[:, bass.ts(rank, NOT)])
            vSX_gm = pr.inc("sync", "SXR", mk_gm, 16)

            oo = out_o.rearrange("(n p) d -> n p d", p=128)
            for tt in range(NOT):
                par = tt % 2
                if tt == 0:
                    pr.wait("tensor", "MM", vMM_rk2)
                    pr.wait("tensor", "VE", pr.ctr["VE"])
                    pr.wait("tensor", "SWD", v_wall)
                if tt >= 2:
                    pr.wait("tensor", "VE", vVE_x1[tt - 2])
                vMM_y = 0
                for ho in range(2):
                    b = 2 * par + ho
                    for c in range(8):
                        pr.op("tensor", lambda e, c=c, tt=tt, ho=ho, b=b: e.matmul(
                            bank(b), lhsT=s_hsf[:, c, 128 * tt:128 * (tt + 1)],
                            rhs=s_we[:, c, 512 * ho:512 * (ho + 1)],
                            start=(c == 0), stop=False))
                    mkb = lambda e, ho=ho, b=b: e.matmul(
                        bank(b), lhsT=s_one[:], rhs=s_be[0:1, 512 * ho:512 * (ho + 1)],
                        start=False, stop=True)
                    if ho == 1:
                        vMM_y = pr.inc("tensor", "MM", mkb)
                    else:
                        pr.op("tensor", mkb)
                pr.wait("vector", "MM", vMM_y)
                pr.wait("vector", "SXR", vSX_gm)
                if tt >= 2:
                    pr.wait("vector", "ST%d" % par, 16 * (tt // 2))
                for ho in range(2):
                    b = 2 * par + ho
                    mk = lambda e, tt=tt, ho=ho, b=b, par=par: e.scalar_tensor_tensor(
                        out=s_out[:, par, 512 * ho:512 * (ho + 1)], in0=bank(b),
                        scalar=s_gm[:, tt:tt + 1],
                        in1=s_x1[:, tt, 512 * ho:512 * (ho + 1)],
                        op0=ALU.mult, op1=ALU.add)
                    if ho == 1:
                        vVE_x1[tt] = pr.inc("vector", "VE", mk)
                    else:
                        pr.op("vector", mk)
                pr.wait("sync", "VE", vVE_x1[tt])
                pr.inc("sync", "ST%d" % par, lambda e, tt=tt, par=par: e.dma_start(
                    out=oo[tt], in_=s_out[:, par]), 16)

            pr.wait("sync", "ST0", None)
            pr.wait("sync", "ST1", None)

        with nc.Block() as blk:
            pr.emit(blk)

    return nc


def host_tables(T):
    pos = np.arange(T, dtype=np.float64)
    i = np.arange(0, DH, 2, dtype=np.float64) / DH
    theta = pos[:, None] / (ROPE_BASE ** i[None, :])
    twopi = 2 * np.pi

    def red(th):
        r = np.mod(th + np.pi, twopi) - np.pi
        hi = r.astype(np.float32)
        lo = (r - hi.astype(np.float64)).astype(np.float32)
        return hi, lo

    a, b = red(theta)
    c, d = red(theta + np.pi / 2)
    return np.stack([a, b, c, d]).astype(np.float32)


def rope_perm():
    p = np.arange(512).reshape(8, 64)
    q = np.concatenate([p[:, 0::2], p[:, 1::2]], axis=1)
    return q.reshape(-1)


def make_inputs(full, T):
    (query, wq, wk, wv, wo, Aq, Bq, Av, Bv, w_gate, w_expert, b_expert) = full
    perm = rope_perm()
    th = host_tables(T)
    tri_i = np.triu(np.ones((128, 128), np.float32), 0)
    trs_i = np.triu(np.ones((128, 128), np.float32), 1)
    nid_i = (np.eye(128, dtype=np.float32) * -1e9)
    one_i = np.ones((1, 128), np.float32)
    cst = np.zeros((128, 16), np.float32)
    cst[:, C_EPS] = 1e-5
    cst[:, C_R1024] = 1.0 / D
    cst[:, C_NHALF] = -0.5
    cst[:, C_R8] = 0.125
    cst[:, C_SHIFT] = -12.0
    cst[:, C_CAP] = 100.5
    cst[:, C_NONE] = -1.0
    maps = []
    for c in range(8):
        b, hh = c // 2, c % 2
        sl = slice(512 * hh, 512 * (hh + 1))
        wq_s = np.ascontiguousarray(wq[:, sl][:, perm])
        wk_s = np.ascontiguousarray(wk[:, sl][:, perm])
        wv_s = np.ascontiguousarray(wv[:, sl])
        bq_s = np.ascontiguousarray(Bq[:, sl][:, perm])
        bv_s = np.ascontiguousarray(Bv[:, sl])
        xb = np.ascontiguousarray(query[b, :T])
        maps.append(dict(
            x_in=xb, x_own=np.ascontiguousarray(xb[hh * (T // 2):(hh + 1) * (T // 2)]),
            wq=wq_s, wk=wk_s, wv=wv_s, wo_i=np.ascontiguousarray(wo),
            aq=np.ascontiguousarray(Aq), av=np.ascontiguousarray(Av),
            bq_i=bq_s, bv_i=bv_s, wg_i=np.ascontiguousarray(w_gate),
            be_i=np.ascontiguousarray(b_expert.reshape(1, D)),
            we_i=np.ascontiguousarray(w_expert),
            th_i=np.ascontiguousarray(
                th[:, :T].reshape(4, T // 128, 128, 32).transpose(2, 0, 1, 3).reshape(128, -1)),
            tri_i=tri_i, trs_i=trs_i, nid_i=nid_i,
            one_i=one_i, cst_i=cst,
        ))
    return maps


def kernel(L0_attn_query, L0_attn_wq, L0_attn_wk, L0_attn_wv, L0_attn_wo,
           L0_attn_Aq, L0_attn_Bq, L0_attn_Av, L0_attn_Bv,
           L0_ffn_w_gate, L0_ffn_w_expert, L0_ffn_b_expert):
    T = L0_attn_query.shape[1]
    full = (np.asarray(L0_attn_query, np.float32), np.asarray(L0_attn_wq, np.float32),
            np.asarray(L0_attn_wk, np.float32), np.asarray(L0_attn_wv, np.float32),
            np.asarray(L0_attn_wo, np.float32), np.asarray(L0_attn_Aq, np.float32),
            np.asarray(L0_attn_Bq, np.float32), np.asarray(L0_attn_Av, np.float32),
            np.asarray(L0_attn_Bv, np.float32), np.asarray(L0_ffn_w_gate, np.float32),
            np.asarray(L0_ffn_w_expert, np.float32),
            np.asarray(L0_ffn_b_expert, np.float32))
    maps = make_inputs(full, T)
    nc = build_nc(T)
    res = run_bass_kernel_spmd(nc, maps, core_ids=list(range(8)))
    out = np.empty((B, T, D), np.float32)
    for c in range(8):
        b, hh = c // 2, c % 2
        out[b, hh * (T // 2):(hh + 1) * (T // 2)] = res.results[c]["out_o"]
    return out
